# version 22
# baseline (speedup 1.0000x reference)
import numpy as np
import concourse.bass as bass
import concourse.mybir as mybir
from concourse.bass_isa import ReduceOp
from concourse.library_config import attn as _LIB_ATTN
from concourse.library_overlay import lower_extended_insts
from concourse.bass_utils import run_bass_kernel_spmd

F32 = mybir.dt.float32
BF16 = mybir.dt.bfloat16
FP8 = mybir.dt.float8e4
AF = mybir.ActivationFunctionType
OP = mybir.AluOpType
AX = mybir.AxisListType

S = 4096
D = 1024
NCH = 32
NT = 8
EPS = 1e-5
ENGS = ("sync", "vector", "scalar", "tensor", "gpsimd")


class Buf:
    __slots__ = ("w", "r")

    def __init__(self):
        self.w = None
        self.r = []


class Ctr:
    def __init__(self, stack, nc, name, is_dma=False):
        self.sem = stack.enter_context(nc.semaphore(name))
        self.n = 0
        self.name = name
        self.is_dma = is_dma


class Builder:
    def __init__(self, nc, stack):
        self.nc = nc
        self.script = []
        self.ctr = {e: Ctr(stack, nc, "c_" + e) for e in ENGS}
        self.dctr = {}
        self.stack = stack
        self.waited = {e: {} for e in ENGS}

    def dma_ctr(self, name):
        if name not in self.dctr:
            self.dctr[name] = Ctr(self.stack, self.nc, "d_" + name, is_dma=True)
        return self.dctr[name]

    def _waits_for(self, eng, reads, writes):
        need = {}
        for b in reads:
            if b.w is not None:
                c, v = b.w
                need[c] = max(need.get(c, 0), v)
        for b in writes:
            if b.w is not None:
                c, v = b.w
                need[c] = max(need.get(c, 0), v)
            for c, v in b.r:
                need[c] = max(need.get(c, 0), v)
        out = []
        for c, v in need.items():
            if c.is_dma:
                v = c.n
            if self.waited[eng].get(c.name, 0) < v:
                out.append((c, v))
                self.waited[eng][c.name] = v
        return out

    def op(self, eng, emit, reads=(), writes=(), dma=None):
        waits = self._waits_for(eng, reads, writes)
        ctr = self.dma_ctr(dma) if dma else self.ctr[eng]
        inc = 16 if dma else 1
        ctr.n += inc
        val = ctr.n

        def fn(engine, waits=waits, emit=emit, ctr=ctr, inc=inc):
            for c, v in waits:
                engine.wait_ge(c.sem, v)
            emit(engine).then_inc(ctr.sem, inc)
        self.script.append((eng, fn))
        for b in writes:
            b.w = (ctr, val)
            b.r = []
        for b in reads:
            b.r.append((ctr, val))
        return (ctr, val)

    def group(self, eng, emits, reads=(), writes=()):
        waits = self._waits_for(eng, reads, writes)
        ctr = self.ctr[eng]
        ctr.n += 1
        val = ctr.n

        def fn(engine, waits=waits, emits=emits, ctr=ctr):
            for c, v in waits:
                engine.wait_ge(c.sem, v)
            ins = None
            for e in emits:
                ins = e(engine)
            ins.then_inc(ctr.sem, 1)
        self.script.append((eng, fn))
        for b in writes:
            b.w = (ctr, val)
            b.r = []
        for b in reads:
            b.r.append((ctr, val))
        return (ctr, val)

    def raw(self, eng, fn):
        self.script.append((eng, fn))

    def barrier(self):
        snap = [(c, c.n) for c in list(self.ctr.values()) + list(self.dctr.values())
                if c.n > 0]
        for e in ENGS:
            waits = [(c, v) for (c, v) in snap
                     if self.waited[e].get(c.name, 0) < v and c is not self.ctr[e]]
            for c, v in waits:
                self.waited[e][c.name] = v

            def fn(engine, waits=waits):
                for c, v in waits:
                    engine.wait_ge(c.sem, v)
            self.script.append((e, fn))


import os as _os
_LOCAL_CC = _os.environ.get("KERN_LOCAL_CC") == "1"


def build_nc():
    nc = bass.Bass("TRN2", target_bir_lowering=False)

    x_h = nc.dram_tensor("x", [S, D], F32, kind="ExternalInput").ap()
    w_h = {}
    for nm in ("wqT", "wkT", "wvT", "woT", "wfT"):
        w_h[nm] = nc.dram_tensor(nm, [D, D], F32, kind="ExternalInput").ap()
    qb_h = nc.dram_tensor("qb8", [128, 8], F32, kind="ExternalInput").ap()
    kb_h = nc.dram_tensor("kb8", [128, 8], F32, kind="ExternalInput").ap()
    vb_h = nc.dram_tensor("vbrow", [1, D], F32, kind="ExternalInput").ap()
    ob_h = nc.dram_tensor("obrow", [1, D], F32, kind="ExternalInput").ap()
    fb_h = nc.dram_tensor("fbrow", [1, D], F32, kind="ExternalInput").ap()
    eyeb_h = nc.dram_tensor("eye_bf", [128, 128], BF16, kind="ExternalInput").ap()
    eyef_h = nc.dram_tensor("eye_f32", [128, 128], F32, kind="ExternalInput").ap()
    ones_h = nc.dram_tensor("ones32", [128, 64], BF16, kind="ExternalInput").ap()
    y_h = nc.dram_tensor("y", [S, D], F32, kind="ExternalOutput").ap()
    x2_h = nc.dram_tensor("x2scr", [S, D], F32, kind="Internal").ap()
    wob_h = nc.dram_tensor("wobf", [8 * 128, D], BF16, kind="Internal").ap()
    wqb_h = nc.dram_tensor("wqbf", [8 * 128, D], BF16, kind="Internal").ap()
    xlt_h = nc.dram_tensor("xlt", [8 * 128, 8 * 512], BF16, kind="Internal").ap()
    ccin_h = nc.dram_tensor("ccin", [1, 1], F32, kind="Internal").ap()
    ccout_h = nc.dram_tensor("ccout", [1, 1], F32, kind="Internal").ap()

    xr = x_h.rearrange("(c p) d -> c p d", p=128)
    x2r = x2_h.rearrange("(c p) d -> c p d", p=128)
    yr = y_h.rearrange("(c p) d -> c p d", p=128)
    wr = {nm: w_h[nm].rearrange("(i p) o -> i p o", p=128) for nm in w_h}
    wob_r = wob_h.rearrange("(i p) o -> i p o", p=128)
    wqb_r = wqb_h.rearrange("(i p) o -> i p o", p=128)
    wqsl_w = wqb_h.rearrange("(oc p) (i q) -> i p oc q", p=128, q=128)
    xlt_r = xlt_h.rearrange("(i p) o -> i p o", p=128)

    from contextlib import ExitStack
    with ExitStack() as st:
        def sb(name, shape, dt=F32):
            return st.enter_context(nc.sbuf_tensor(name, shape, dt))

        B = Builder(nc, st)

        XFR = sb("XFR", [128, 2 * D], F32)
        stats = {}
        for nm in ("s1", "s2", "r1", "n1", "r2", "n2", "r3", "n3", "r4", "n4"):
            stats[nm] = sb("st_" + nm, [128, NCH], F32)
        qb8 = sb("qb8s", [128, 8], F32)
        kb8 = sb("kb8s", [128, 8], F32)
        bbc = sb("bbc", [128, D], F32)
        eye_bf = sb("eye_bf_s", [128, 128], BF16)
        ones32 = sb("ones32_s", [128, 64], BF16)
        amax_r = sb("amax_r", [128, 1], F32)
        amax_a = sb("amax_a", [128, 1], F32)
        wmax_r = sb("wmax_r", [128, 1], F32)
        sc_t = sb("sc_t", [1, 8], F32)
        shx_bc = sb("shx_bc", [128, 1], F32)
        d_bc = sb("d_bc", [128, 1], F32)

        from contextlib import ExitStack as _ES
        stAB = _ES()

        def sbA(name, shape, dt=F32):
            return stAB.enter_context(nc.sbuf_tensor(name, shape, dt))

        KT = sbA("KT", [128, 8 * S], BF16)
        V = sbA("V", [128, NCH * D], BF16)
        QT = sbA("QT", [128, 8 * 512], BF16)
        CTXT = sbA("CTXT", [128, 2 * 8 * 512], BF16)
        XLNT = sbA("XLNT", [128, 8 * 512], BF16)
        EXPT = sbA("EXPT", [128, 3 * 1024], BF16)
        XFP = sbA("XFP", [128, D], F32)
        WOS = sbA("WOS", [128, 2 * D], BF16)
        RBC = sbA("RBC", [128, 512], F32)

        bufs = {}

        def buf(key):
            if key not in bufs:
                bufs[key] = Buf()
            return bufs[key]

        def dma(eng_name, out_ap, in_ap, key, qname, extra_reads=()):
            B.op(eng_name, lambda e, o=out_ap, i=in_ap: e.dma_start(out=o, in_=i),
                 reads=[buf(k) for k in extra_reads], writes=[buf(key)], dma=qname)

        dma("sync", qb8[:], qb_h, "qb8", "pre")
        dma("sync", kb8[:], kb_h, "kb8", "pre")
        dma("sync", eye_bf[:], eyeb_h, "eyeb", "pre")
        dma("sync", ones32[:], ones_h, "ones", "pre")
        B.raw("gpsimd", lambda g: g.load_library(_LIB_ATTN))
        B.op("sync", lambda e: e.dma_start(out=XFP[0:1, :], in_=vb_h),
             reads=[buf("XFP")], writes=[buf("XFP")], dma="prow")
        B.op("gpsimd", lambda g: g.partition_broadcast(bbc[:], XFP[0:1, :], channels=128),
             reads=[buf("XFP")], writes=[buf("bbc")])
        B.op("vector", lambda v: v.memset(amax_r[:], 0.0), writes=[buf("amax_r")])
        B.op("vector", lambda v: v.memset(wmax_r[:], 0.0), writes=[buf("wmax_r")])

        def warm_burst(ps, key):
            B.group("tensor", [
                (lambda tt, i=i, ps=ps: tt.matmul(
                    ps[:, 0:128], eye_bf[:], eye_bf[:],
                    start=(i == 0), stop=(i == 31)))
                for i in range(32)],
                reads=[buf("eyeb")], writes=[buf(key)])

        def load_wbig(nm):
            for ic in range(8):
                dma("gpsimd", WBIG[:, ic * D:(ic + 1) * D], wr[nm][ic],
                    ("wbig", ic), "wq1")

        def wbig_read(ics):
            return [buf(("wbig", ic)) for ic in ics]

        def dump_ap(slot):
            return XFR[:, slot * D:(slot + 1) * D]

        def ln_stats(xap, col, xkey, dslot, dump=None, dkey=None):
            B.op("vector", lambda v, a=xap, c=col: v.tensor_reduce(
                stats["s1"][:, c:c + 1], a, AX.X, OP.add),
                reads=[buf(xkey)], writes=[buf(("s1", col))])
            if dump is None:
                dump, dkey = dump_ap(dslot), ("xfr", dslot)
            B.op("vector", lambda v, a=xap, c=col, dump=dump: v.scalar_tensor_tensor(
                dump, a, 1.0, a, OP.mult, OP.mult,
                accum_out=stats["s2"][:, c:c + 1]),
                reads=[buf(xkey)], writes=[buf(dkey), buf(("s2", col))])

        def ln_chain(c0, n, rk, nk):
            sl = slice(c0, c0 + n)
            sbufs = [buf(("s1", c)) for c in range(c0, c0 + n)] + \
                    [buf(("s2", c)) for c in range(c0, c0 + n)]
            nb = [buf((nk, c)) for c in range(c0, c0 + n)]
            rb = [buf((rk, c)) for c in range(c0, c0 + n)]
            tb = [buf(("s2", c)) for c in range(c0, c0 + n)]
            B.op("vector", lambda v: v.tensor_scalar(
                stats[nk][:, sl], stats["s1"][:, sl], 1.0 / D, None, OP.mult),
                reads=sbufs[:n], writes=nb)
            B.op("vector", lambda v: v.tensor_scalar(
                stats["s2"][:, sl], stats["s2"][:, sl], 1.0 / D, None, OP.mult),
                reads=sbufs[n:], writes=tb)
            B.op("vector", lambda v: v.tensor_tensor(
                stats[rk][:, sl], stats[nk][:, sl], stats[nk][:, sl], OP.mult),
                reads=nb, writes=rb)
            B.op("vector", lambda v: v.tensor_tensor(
                stats["s2"][:, sl], stats["s2"][:, sl], stats[rk][:, sl], OP.subtract),
                reads=tb + rb, writes=tb)
            B.op("vector", lambda v: v.tensor_scalar(
                stats["s2"][:, sl], stats["s2"][:, sl], EPS, None, OP.add),
                reads=tb, writes=tb)
            B.op("scalar", lambda s: s.activation(
                stats["s2"][:, sl], stats["s2"][:, sl], AF.Ln),
                reads=tb, writes=tb)
            B.op("vector", lambda v: v.tensor_scalar(
                stats["s2"][:, sl], stats["s2"][:, sl], -0.5, None, OP.mult),
                reads=tb, writes=tb)
            B.op("scalar", lambda s: s.activation(
                stats[rk][:, sl], stats["s2"][:, sl], AF.Exp),
                reads=tb, writes=rb)
            B.op("vector", lambda v: v.scalar_tensor_tensor(
                stats[nk][:, sl], stats[nk][:, sl], -1.0, stats[rk][:, sl],
                OP.mult, OP.mult),
                reads=nb + rb, writes=nb)

        def chain_dve(ch, rk, nk):
            sl = slice(ch, ch + 1)
            sb1 = [buf(("s1", ch))]
            sb2 = [buf(("s2", ch))]
            nb = [buf((nk, ch))]
            rb = [buf((rk, ch))]
            B.op("vector", lambda v: v.tensor_scalar(
                stats[nk][:, sl], stats["s1"][:, sl], 1.0 / D, None, OP.mult),
                reads=sb1, writes=nb)
            B.op("vector", lambda v: v.tensor_scalar(
                stats["s2"][:, sl], stats["s2"][:, sl], 1.0 / D, None, OP.mult),
                reads=sb2, writes=sb2)
            B.op("vector", lambda v: v.tensor_tensor(
                stats[rk][:, sl], stats[nk][:, sl], stats[nk][:, sl], OP.mult),
                reads=nb, writes=rb)
            B.op("vector", lambda v: v.tensor_tensor(
                stats["s2"][:, sl], stats["s2"][:, sl], stats[rk][:, sl], OP.subtract),
                reads=sb2 + rb, writes=sb2)
            B.op("vector", lambda v: v.tensor_scalar(
                stats["s2"][:, sl], stats["s2"][:, sl], EPS, None, OP.add),
                reads=sb2, writes=sb2)

        def chain_act(ch, rk):
            sl = slice(ch, ch + 1)
            B.op("scalar", lambda s: s.activation(
                stats["s2"][:, sl], stats["s2"][:, sl], AF.Ln),
                reads=[buf(("s2", ch))], writes=[buf(("s2", ch))])
            B.op("scalar", lambda s: s.activation(
                stats[rk][:, sl], stats["s2"][:, sl], AF.Exp, scale=-0.5),
                reads=[buf(("s2", ch))], writes=[buf((rk, ch))])

        def chain_post(ch, rk, nk):
            sl = slice(ch, ch + 1)
            B.op("vector", lambda v: v.scalar_tensor_tensor(
                stats[nk][:, sl], stats[nk][:, sl], -1.0, stats[rk][:, sl],
                OP.mult, OP.mult),
                reads=[buf((nk, ch)), buf((rk, ch))], writes=[buf((nk, ch))])

        def apply_transpose(ch, ps_tr, ps_key, xa=None, xkey="XFP"):
            if xa is None:
                xa = XFP[:]
            j4 = ch % 4
            B.op("vector", lambda v, c=ch, xa=xa: v.tensor_scalar(
                EXPT[:, 0:D], xa, stats["r1"][:, c:c + 1],
                stats["n1"][:, c:c + 1], OP.mult, OP.add),
                reads=[buf(xkey), buf(("r1", ch)), buf(("n1", ch))],
                writes=[buf(("expt", 0))])
            B.group("tensor", [
                (lambda t, ic=ic, ps_tr=ps_tr: t.transpose(
                    ps_tr[:, ic * 128:(ic + 1) * 128],
                    EXPT[:, ic * 128:(ic + 1) * 128], eye_bf[:]))
                for ic in range(8)],
                reads=[buf(("expt", 0)), buf("eyeb")],
                writes=[buf(ps_key)])
            xv = XLNT[:, :].rearrange("p (ic q) -> p ic q", ic=8)[:, :, j4 * 128:(j4 + 1) * 128]
            pv = ps_tr[:, :].rearrange("p (ic q) -> p ic q", ic=8)
            B.op("vector", lambda v, xv=xv, pv=pv: v.tensor_copy(xv, pv),
                 reads=[buf(ps_key)], writes=[buf(("xlnt", j4))])

        with ExitStack() as stA:
            ps_trA = stA.enter_context(nc.psum_tensor("ps_trA", [128, 1024], BF16))
            ps_kv0 = stA.enter_context(nc.psum_tensor("ps_kv0", [128, 512], F32))
            ps_kv1 = stA.enter_context(nc.psum_tensor("ps_kv1", [128, 512], F32))
            ps_kv = [ps_kv0, ps_kv1]
            WBIG = stA.enter_context(nc.sbuf_tensor("WBIG", [128, 8 * D], BF16))

            load_wbig("wkT")
            warm_burst(ps_kv0, ("ps_kv", 0))

            for nm, k in (("woT", "wob"), ("wqT", "wqb")):
                for ic in range(8):
                    fsl = XFP[:] if ic % 2 == 0 else XFR[:, 0:D]
                    fkey = "XFP" if ic % 2 == 0 else ("xfr", 0)
                    dma("sync", fsl, wr[nm][ic], fkey, "xq0")
                    B.op("vector", lambda v, fsl=fsl: v.tensor_copy(WOS[:, 0:D], fsl),
                         reads=[buf(fkey)], writes=[buf(("wos", 0))])
                    if k == "wob":
                        for oc2 in range(2):
                            blk = wob_r[oc2 * 4 + ic // 2]
                            dma("sync", blk[:, (ic % 2) * 512:(ic % 2) * 512 + 512],
                                WOS[:, oc2 * 512:(oc2 + 1) * 512],
                                ("wob", ic), "pcw", extra_reads=[("wos", 0)])
                    else:
                        dma("sync", wqsl_w[ic],
                            WOS[:, 0:D].rearrange("p (oc q) -> p oc q", q=128),
                            ("wqb", ic), "pcw", extra_reads=[("wos", 0)])

            def xslot(ch):
                if ch % 2 == 0:
                    return (XFP[:], "XFP", "xq0")
                return (XFR[:, 0:D], ("xfr", 0), "xq1")

            def stats_block(s_i):
                for j4 in range(4):
                    ch = 4 * s_i + j4
                    xa, xkey, xqn = xslot(ch)
                    dma("sync", xa, xr[ch], xkey, xqn)
                    ln_stats(xa, ch, xkey, (ch + 1) % 2)
                ln_chain(4 * s_i, 4, "r1", "n1")

            def apply_block(s_i):
                for j4 in range(4):
                    ch = 4 * s_i + j4
                    xa, xkey, xqn = xslot(ch)
                    dma("sync", xa, xr[ch], xkey, xqn)
                    apply_transpose(ch, ps_trA, "ps_trA", xa, xkey)
                B.op("sync", lambda e, s_i=s_i: e.dma_start(
                    out=xlt_r[s_i], in_=XLNT[:]),
                    reads=[buf(("xlnt", j)) for j in range(4)],
                    writes=[buf(("xlt", s_i))], dma="xlw")

            def kproj_block(s_i):
                for oc in range(8):
                    pk = ps_kv[oc % 2]
                    B.group("tensor", [
                        (lambda t, ic=ic, pk=pk, oc=oc: t.matmul(
                            pk[:], WBIG[:, ic * D + oc * 128: ic * D + oc * 128 + 128],
                            XLNT[:, ic * 512:(ic + 1) * 512],
                            start=(ic == 0), stop=(ic == 7)))
                        for ic in range(8)],
                        reads=wbig_read(range(8)) + [buf(("xlnt", j)) for j in range(4)],
                        writes=[buf(("ps_kv", oc % 2))])
                    B.op("vector", lambda v, oc=oc, s_i=s_i, pk=pk: v.tensor_scalar(
                        KT[:, oc * S + s_i * 512: oc * S + s_i * 512 + 512],
                        pk[:], kb8[:, oc:oc + 1], None, OP.add),
                        reads=[buf(("ps_kv", oc % 2)), buf("kb8")],
                        writes=[buf(("KT", oc, s_i))])

            for s_i in range(NT):
                stats_block(s_i)
                apply_block(s_i)
                kproj_block(s_i)

            load_wbig("wvT")
            for s_i in range(NT):
                B.op("sync", lambda e, s_i=s_i: e.dma_start(
                    out=XLNT[:], in_=xlt_r[s_i]),
                    reads=[buf(("xlt", s_i))],
                    writes=[buf(("xlnt", j)) for j in range(4)], dma="xlr")
                for ov in range(2):
                    for tc in range(4):
                        pk = ps_kv[tc % 2]
                        B.group("tensor", [
                            (lambda t, ic=ic, tc=tc, pk=pk, ov=ov: t.matmul(
                                pk[:],
                                XLNT[:, ic * 512 + tc * 128: ic * 512 + (tc + 1) * 128],
                                WBIG[:, ic * D + ov * 512: ic * D + ov * 512 + 512],
                                start=(ic == 0), stop=(ic == 7)))
                            for ic in range(8)],
                            reads=wbig_read(range(8)) + [buf(("xlnt", tc))],
                            writes=[buf(("ps_kv", tc % 2))])
                        ch = 4 * s_i + tc
                        B.op("vector", lambda v, ch=ch, ov=ov, pk=pk: v.tensor_tensor(
                            V[:, ch * D + ov * 512: ch * D + ov * 512 + 512],
                            pk[:], bbc[:, ov * 512:(ov + 1) * 512], OP.add),
                            reads=[buf(("ps_kv", tc % 2)), buf("bbc")],
                            writes=[buf(("V", ch, ov))])

        for ic in range(8):
            fsl = XFP[:] if ic % 2 == 0 else XFR[:, 0:D]
            fkey = "XFP" if ic % 2 == 0 else ("xfr", 0)
            dma("sync", fsl, wr["wfT"][ic], fkey, "wf%d" % (ic % 2))
            B.op("vector", lambda v, fsl=fsl: v.tensor_reduce(
                amax_a[:], fsl, AX.X, OP.max, apply_absolute_value=True),
                reads=[buf(fkey)], writes=[buf("amax_a")])
            B.op("vector", lambda v: v.tensor_tensor(
                wmax_r[:], wmax_r[:], amax_a[:], OP.max),
                reads=[buf("wmax_r"), buf("amax_a")], writes=[buf("wmax_r")])
        B.op("gpsimd", lambda g: g.partition_all_reduce(
            wmax_r[:], wmax_r[:], channels=128, reduce_op=ReduceOp.max),
            reads=[buf("wmax_r")], writes=[buf("wmax_r")])

        onec = sbA("onec", [1, 128], F32)
        WQS = sbA("WQS", [128, D], BF16)
        B.op("vector", lambda v: v.memset(onec[:], 1.0), writes=[buf("onec")])
        B.barrier()
        B.op("sync", lambda e: e.dma_start(out=bbc[0:1, :], in_=ob_h),
             reads=[buf("bbc")], writes=[buf("bbc")], dma="prow")

        with ExitStack() as stB:
            ps_sc = stB.enter_context(nc.psum_tensor("ps_sc", [128, 2048], F32))
            ps_ctx = [stB.enter_context(nc.psum_tensor(f"ps_ctx{i}", [128, 512], F32))
                      for i in range(2)]
            ps_sum = stB.enter_context(nc.psum_tensor("ps_sum", [128, 512], F32))
            ps_op = stB.enter_context(nc.psum_tensor("ps_op", [128, 512], F32))

            def prep_x(t):
                B.op("sync", lambda e, t=t: e.dma_start(out=XLNT[:], in_=xlt_r[t]),
                     reads=[buf(("xlt", t))],
                     writes=[buf(("xlnt", j)) for j in range(4)], dma="xlr")

            def prep_oc_steps(t, oc, ps, pkey):
                def load():
                    dma("sync", WQS[:], wqb_r[oc], "WQS", "wqsl",
                        extra_reads=[("wqb", i) for i in range(8)])

                def mms():
                    B.group("tensor", [
                        (lambda tt, ic=ic, ps=ps: tt.matmul(
                            ps[:], WQS[:, ic * 128:(ic + 1) * 128],
                            XLNT[:, ic * 512:(ic + 1) * 512],
                            start=(ic == 0), stop=(ic == 7)))
                        for ic in range(8)],
                        reads=[buf("WQS")] + [buf(("xlnt", j)) for j in range(4)],
                        writes=[buf(pkey)])

                def evac():
                    B.op("vector", lambda v, oc=oc, ps=ps: v.tensor_scalar(
                        QT[:, oc * 512:(oc + 1) * 512], ps[:],
                        qb8[:, oc:oc + 1], 0.125, OP.add, OP.mult),
                        reads=[buf(pkey), buf("qb8")],
                        writes=[buf(("QT", oc))])
                return [load, mms, evac]

            def prep_oc(t, oc, ps, pkey):
                for s in prep_oc_steps(t, oc, ps, pkey):
                    s()

            def prep_mm(t):
                for oc in range(8):
                    prep_oc(t, oc, ps_ctx[oc % 2], ("ps_ctx", oc % 2))

            def fill_pass_steps(tp, li, oc2):
                ch = 4 * tp + li
                half = tp % 2
                rslot = ch % 2
                xa = XFR[:, rslot * D:(rslot + 1) * D]
                steps = []
                if oc2 == 0:
                    steps.append(lambda: dma(
                        "sync", xa, xr[ch], ("xfr", rslot), "xr%d" % rslot))

                def mk_load(pc2):
                    ws = pc2 % 2
                    def em():
                        dma("sync", WOS[:, ws * D:(ws + 1) * D],
                            wob_r[oc2 * 4 + pc2], ("wos", ws), "wosf%d" % ws,
                            extra_reads=[("wob", 2 * pc2), ("wob", 2 * pc2 + 1)])
                    return em

                def mk_mms(pc2):
                    ws = pc2 % 2

                    def emit():
                        for h2 in range(2):
                            dc = 2 * pc2 + h2
                            B.group("tensor", [
                                (lambda tt, dc=dc, h2=h2, ws=ws: tt.matmul(
                                    ps_op[:],
                                    CTXT[:, half * 4096 + dc * 512 + li * 128:
                                         half * 4096 + dc * 512 + li * 128 + 128],
                                    WOS[:, ws * D + h2 * 512: ws * D + h2 * 512 + 512],
                                    start=(dc == 0), stop=False))],
                                reads=[buf(("wos", ws)), buf(("ctxt", half, dc))],
                                writes=[buf("ps_op")])
                    return emit

                steps.append(mk_load(0))
                for pc2 in range(3):
                    steps.append(lambda pc2=pc2: (mk_mms(pc2)(), mk_load(pc2 + 1)()))

                def last_mms():
                    mk_mms(3)()
                    B.group("tensor", [
                        (lambda tt: tt.matmul(
                            ps_op[:], onec[:], bbc[0:1, oc2 * 512:(oc2 + 1) * 512],
                            start=False, stop=True))],
                        reads=[buf("onec"), buf("bbc")], writes=[buf("ps_op")])
                steps.append(last_mms)

                def copy_stats():
                    xh = XFP[:, oc2 * 512:(oc2 + 1) * 512]
                    B.op("vector", lambda v, xh=xh: v.tensor_copy(xh, ps_op[:]),
                         reads=[buf("ps_op")], writes=[buf("XFP")])
                    if oc2 == 0:
                        B.op("vector", lambda v, ch=ch: v.tensor_reduce(
                            stats["s1"][:, ch:ch + 1], ps_op[:], AX.X, OP.add),
                            reads=[buf("ps_op")], writes=[buf(("s1", ch))])
                        B.op("vector", lambda v, ch=ch, xh=xh:
                             v.scalar_tensor_tensor(
                                WOS[:, 0:512], ps_op[:], 1.0, xh, OP.mult, OP.mult,
                                accum_out=stats["s2"][:, ch:ch + 1]),
                            reads=[buf("ps_op"), buf("XFP")],
                            writes=[buf(("wos", 0)), buf(("s2", ch))])
                    else:
                        B.op("vector", lambda v: v.tensor_reduce(
                            amax_a[:], ps_op[:], AX.X, OP.add),
                            reads=[buf("ps_op")], writes=[buf("amax_a")])
                        B.op("vector", lambda v, ch=ch: v.tensor_tensor(
                            stats["s1"][:, ch:ch + 1], stats["s1"][:, ch:ch + 1],
                            amax_a[:], OP.add),
                            reads=[buf(("s1", ch)), buf("amax_a")],
                            writes=[buf(("s1", ch))])
                        B.op("vector", lambda v, ch=ch, xh=xh:
                             v.scalar_tensor_tensor(
                                WOS[:, 0:512], ps_op[:], 1.0, xh, OP.mult, OP.mult,
                                accum_out=stats["n4"][:, ch:ch + 1]),
                            reads=[buf("ps_op"), buf("XFP")],
                            writes=[buf(("wos", 0)), buf(("n4", ch))])
                        B.op("vector", lambda v, ch=ch: v.tensor_tensor(
                            stats["s2"][:, ch:ch + 1], stats["s2"][:, ch:ch + 1],
                            stats["n4"][:, ch:ch + 1], OP.add),
                            reads=[buf(("s2", ch)), buf(("n4", ch))],
                            writes=[buf(("s2", ch))])
                steps.append(copy_stats)
                return steps

            def fill_k1_chain(tp, li):
                ch = 4 * tp + li
                chain_dve(ch, "r2", "n2")
                chain_act(ch, "r2")

            def fill_k2(tp, li):
                ch = 4 * tp + li
                rslot = ch % 2
                xa = XFR[:, rslot * D:(rslot + 1) * D]
                chain_post(ch, "r2", "n2")
                B.op("vector", lambda v, ch=ch, xa=xa: v.tensor_scalar(
                    xa, xa, stats["n2"][:, ch:ch + 1], None, OP.add),
                    reads=[buf(("xfr", rslot)), buf(("n2", ch))],
                    writes=[buf(("xfr", rslot))])
                for oc2 in range(2):
                    B.op("vector", lambda v, ch=ch, xa=xa, oc2=oc2:
                         v.scalar_tensor_tensor(
                            xa[:, oc2 * 512:(oc2 + 1) * 512],
                            XFP[:, oc2 * 512:(oc2 + 1) * 512],
                            stats["r2"][:, ch:ch + 1],
                            xa[:, oc2 * 512:(oc2 + 1) * 512],
                            OP.mult, OP.add),
                        reads=[buf("XFP"), buf(("r2", ch)), buf(("xfr", rslot))],
                        writes=[buf(("xfr", rslot))])
                dma("sync", x2r[ch], xa, ("x2w", ch), "x2w",
                    extra_reads=[("xfr", rslot)])
                B.op("vector", lambda v, ch=ch, xa=xa: v.tensor_reduce(
                    stats["s1"][:, ch:ch + 1], xa, AX.X, OP.add),
                    reads=[buf(("xfr", rslot))], writes=[buf(("s1", ch))])
                B.op("vector", lambda v, ch=ch, xa=xa: v.scalar_tensor_tensor(
                    WOS[:, 0:D], xa, 1.0, xa, OP.mult, OP.mult,
                    accum_out=stats["s2"][:, ch:ch + 1]),
                    reads=[buf(("xfr", rslot))],
                    writes=[buf(("wos", 0)), buf(("s2", ch))])
                chain_dve(ch, "r3", "n3")
                chain_act(ch, "r3")

            def fill_k3(tp, li):
                ch = 4 * tp + li
                rslot = ch % 2
                xa = XFR[:, rslot * D:(rslot + 1) * D]
                chain_post(ch, "r3", "n3")
                B.op("vector", lambda v, ch=ch, xa=xa: v.tensor_reduce(
                    stats["r4"][:, ch:ch + 1], xa, AX.X, OP.max),
                    reads=[buf(("xfr", rslot))], writes=[buf(("r4", ch))])
                B.op("vector", lambda v, ch=ch, xa=xa: v.tensor_reduce(
                    stats["n4"][:, ch:ch + 1], xa, AX.X, OP.min),
                    reads=[buf(("xfr", rslot))], writes=[buf(("n4", ch))])
                B.op("vector", lambda v, ch=ch: v.tensor_scalar(
                    stats["r4"][:, ch:ch + 1], stats["r4"][:, ch:ch + 1],
                    stats["r3"][:, ch:ch + 1], stats["n3"][:, ch:ch + 1],
                    OP.mult, OP.add),
                    reads=[buf(("r4", ch)), buf(("r3", ch)), buf(("n3", ch))],
                    writes=[buf(("r4", ch))])
                B.op("vector", lambda v, ch=ch: v.tensor_scalar(
                    stats["n4"][:, ch:ch + 1], stats["n4"][:, ch:ch + 1],
                    stats["r3"][:, ch:ch + 1], stats["n3"][:, ch:ch + 1],
                    OP.mult, OP.add),
                    reads=[buf(("n4", ch)), buf(("r3", ch)), buf(("n3", ch))],
                    writes=[buf(("n4", ch))])
                B.op("vector", lambda v, ch=ch: v.tensor_scalar(
                    stats["n4"][:, ch:ch + 1], stats["n4"][:, ch:ch + 1],
                    -1.0, None, OP.mult),
                    reads=[buf(("n4", ch))], writes=[buf(("n4", ch))])
                B.op("vector", lambda v, ch=ch: v.tensor_tensor(
                    amax_a[:], stats["r4"][:, ch:ch + 1],
                    stats["n4"][:, ch:ch + 1], OP.max),
                    reads=[buf(("r4", ch)), buf(("n4", ch))],
                    writes=[buf("amax_a")])
                B.op("vector", lambda v: v.tensor_tensor(
                    amax_r[:], amax_r[:], amax_a[:], OP.max),
                    reads=[buf("amax_r"), buf("amax_a")], writes=[buf("amax_r")])

            def fill_stage(tp, li, k):
                if k <= 1:
                    for s in fill_pass_steps(tp, li, k):
                        s()
                    if k == 1:
                        fill_k1_chain(tp, li)
                elif k == 2:
                    fill_k2(tp, li)
                else:
                    fill_k3(tp, li)

            def fill_at(tp, c):
                for li in range(4):
                    k = c - 2 * li
                    if 0 <= k <= 3:
                        pass
                for li, k in sorted(
                        [(li, c - 2 * li) for li in range(4)
                         if 0 <= c - 2 * li <= 3],
                        key=lambda x: -x[1]):
                    fill_stage(tp, li, k)

            prep_x(0)
            prep_mm(0)
            warm_burst(ps_op, "ps_op")
            def scores_emits_f(kk, c):
                slot = kk % 2
                return [
                    (lambda tt, kk=kk, c=c, slot=slot: tt.matmul(
                        ps_sc[:, slot * 1024: slot * 1024 + 512],
                        KT[0:64, c * S + kk * 128: c * S + kk * 128 + 128],
                        QT[0:64, c * 512:(c + 1) * 512],
                        start=True, stop=True, tile_position=(0, 0))),
                    (lambda tt, kk=kk, c=c, slot=slot: tt.matmul(
                        ps_sc[:, slot * 1024 + 512: slot * 1024 + 1024],
                        KT[64:128, c * S + kk * 128: c * S + kk * 128 + 128],
                        QT[64:128, c * 512:(c + 1) * 512],
                        start=True, stop=True, tile_position=(64, 0)))]

            def exp_op_f(kk):
                slot = kk % 2
                es = kk % 3
                B.op("scalar", lambda s, slot=slot, es=es: s.activation(
                    EXPT[:, es * 1024:(es + 1) * 1024],
                    ps_sc[:, slot * 1024:(slot + 1) * 1024], AF.Exp),
                    reads=[buf(("ps_sc", 2 * slot)), buf(("ps_sc", 2 * slot + 1))],
                    writes=[buf(("expt", es))])

            def head(t, c):
                for kk in range(3):
                    B.group("tensor", scores_emits_f(kk, c),
                            reads=[buf(("KT", c, 0)), buf(("QT", c))],
                            writes=[buf(("ps_sc", 2 * (kk % 2))),
                                    buf(("ps_sc", 2 * (kk % 2) + 1))])
                    exp_op_f(kk)

            for t in range(NT):
                for c in range(8):
                    pc = ps_ctx[c % 2]
                    cb = buf(("ps_ctx", c % 2))

                    def scores_emits(kk, c=c):
                        return scores_emits_f(kk, c)

                    def exp_op(kk):
                        exp_op_f(kk)

                    if t == 0 and c == 0:
                        head(0, 0)
                    if t + 1 < NT and c == 0:
                        prep_x(t + 1)
                    fillers = []
                    prep_evac = None
                    if t > 0:
                        fillers += fill_pass_steps(t - 1, c // 2, c % 2)
                    if t + 1 < NT:
                        psteps = prep_oc_steps(t + 1, c, ps_op, "ps_op")
                        fillers += psteps[:2]
                        prep_evac = psteps[2]
                    for kk in range(NCH):
                        slot = kk % 2
                        eb = buf(("expt", kk % 3))
                        if kk == 0 and t > 0 and c >= 2 and c % 2 == 0:
                            fill_k2(t - 1, (c - 2) // 2)
                        if kk >= 6 and kk % 2 == 0 and (kk - 6) // 2 < len(fillers):
                            fillers[(kk - 6) // 2]()
                        if kk + 3 < NCH:
                            B.group("tensor", scores_emits(kk + 3),
                                    reads=[buf(("KT", c, (kk + 3) // 4)), buf(("QT", c))],
                                    writes=[buf(("ps_sc", 2 * ((kk + 3) % 2))),
                                            buf(("ps_sc", 2 * ((kk + 3) % 2) + 1))])
                        es = kk % 3
                        emits = [
                            (lambda tt, kk=kk, c=c, es=es, pc=pc: tt.matmul(
                                pc[0:64, :],
                                V[:, kk * D + c * 128: kk * D + c * 128 + 64],
                                EXPT[:, es * 1024: es * 1024 + 512],
                                start=(kk == 0), stop=(kk == NCH - 1),
                                tile_position=(0, 0), skip_group_check=True)),
                            (lambda tt, kk=kk, c=c, es=es, pc=pc: tt.matmul(
                                pc[64:128, :],
                                V[:, kk * D + c * 128 + 64: kk * D + c * 128 + 128],
                                EXPT[:, es * 1024 + 512: es * 1024 + 1024],
                                start=(kk == 0), stop=(kk == NCH - 1),
                                tile_position=(0, 64), skip_group_check=True)),
                            (lambda tt, kk=kk, es=es: tt.matmul(
                                ps_sum[0:64, :], ones32[:],
                                EXPT[:, es * 1024: es * 1024 + 512],
                                start=(kk == 0), stop=(kk == NCH - 1),
                                tile_position=(0, 0), skip_group_check=True)),
                            (lambda tt, kk=kk, es=es: tt.matmul(
                                ps_sum[64:128, :], ones32[:],
                                EXPT[:, es * 1024 + 512: es * 1024 + 1024],
                                start=(kk == 0), stop=(kk == NCH - 1),
                                tile_position=(0, 64), skip_group_check=True))]
                        rd = [eb, buf(("V", kk, 0)), buf(("V", kk, 1)), buf("ones")]
                        wrt = [cb, buf("ps_sum")]
                        B.group("tensor", emits, reads=rd, writes=wrt)
                        if kk + 3 < NCH:
                            exp_op(kk + 3)
                    B.op("vector", lambda v: v.tensor_copy(RBC[:], ps_sum[:]),
                         reads=[buf("ps_sum")], writes=[buf("RBC")])
                    B.op("vector", lambda v: v.reciprocal_approx_fast(RBC[:], RBC[:]),
                         reads=[buf("RBC")], writes=[buf("RBC")])
                    B.op("vector", lambda v, c=c, pc=pc, t=t: v.tensor_tensor(
                        CTXT[0:64, (t % 2) * 4096 + c * 512:
                             (t % 2) * 4096 + (c + 1) * 512], pc[0:64, :],
                        RBC[0:64, :], OP.mult),
                        reads=[cb, buf("RBC")], writes=[buf(("ctxt", t % 2, c))])
                    B.op("vector", lambda v, c=c, pc=pc, t=t: v.tensor_tensor(
                        CTXT[64:128, (t % 2) * 4096 + c * 512:
                             (t % 2) * 4096 + (c + 1) * 512], pc[64:128, :],
                        RBC[64:128, :], OP.mult),
                        reads=[cb, buf("RBC")], writes=[buf(("ctxt", t % 2, c))])
                    if c + 1 < 8:
                        head(t, c + 1)
                    elif t + 1 < NT:
                        head(t + 1, 0)
                    if prep_evac is not None:
                        prep_evac()
                    if t > 0 and c % 2 == 1:
                        fill_k1_chain(t - 1, c // 2)
                        if c >= 3:
                            fill_k3(t - 1, (c - 3) // 2)
                if t > 0:
                    fill_stage(t - 1, 3, 2)
                    fill_stage(t - 1, 3, 3)
            for stp in range(11):
                for li in range(4):
                    k = stp - 2 * li
                    if 0 <= k <= 3:
                        fill_stage(NT - 1, li, k)

        B.barrier()
        B.op("sync", lambda e: e.dma_start(out=XFP[0:1, :], in_=fb_h),
             reads=[buf("XFP")], writes=[buf("XFP")], dma="prow")
        B.op("gpsimd", lambda g: g.partition_broadcast(bbc[:], XFP[0:1, :], channels=128),
             reads=[buf("XFP"), buf("bbc")], writes=[buf("bbc")])
        stAB.close()
        SCR = sb("SCRC", [128, D], F32)
        eye_f32 = sb("eye_f32_s", [128, 128], F32)
        dma("sync", eye_f32[:], eyef_h, "eyef", "eyefq")
        WFQ = sb("WFQ", [128, 8 * D], FP8)
        WFF = sb("WFF", [128, D], F32)
        Q1 = sb("Q1", [128, D], F32)
        XQT2 = sb("XQT2", [128, 2 * D], FP8)
        O2R = sb("O2R", [128, 2 * D], BF16)
        X2C = sb("X2C", [128, 4 * D], F32)
        DMP = sb("DMP", [128, D], F32)
        shw_bc = sb("shw_bc", [128, 1], F32)

        with ExitStack() as stC:
            psT = stC.enter_context(nc.psum_tensor("psT", [128, 1024], F32))
            psF = [stC.enter_context(nc.psum_tensor(f"psF{i}", [128, 512], F32))
                   for i in range(4)]
            warm_burst(psF[0], ("psF", 0))

            B.op("vector", lambda v: v.memset(sc_t[:], 0.0), writes=[buf("sc")])
            B.op("vector", lambda v: v.tensor_scalar(
                sc_t[:, 7:8], wmax_r[0:1, :], 1e-12, None, OP.max),
                reads=[buf("wmax_r"), buf("sc")], writes=[buf("sc")])
            B.op("vector", lambda v: v.reciprocal(sc_t[:, 6:7], sc_t[:, 7:8]),
                 reads=[buf("sc")], writes=[buf("sc")])
            B.op("vector", lambda v: v.tensor_scalar(
                sc_t[:, 3:4], sc_t[:, 6:7], 448.0, 448.0, OP.mult, OP.min),
                reads=[buf("sc")], writes=[buf("sc")])
            B.op("vector", lambda v: v.tensor_scalar(
                sc_t[:, 4:5], sc_t[:, 3:4], 0.5, None, OP.mult),
                reads=[buf("sc")], writes=[buf("sc")])
            B.op("vector", lambda v: v.reciprocal(sc_t[:, 5:6], sc_t[:, 3:4]),
                 reads=[buf("sc")], writes=[buf("sc")])
            B.op("gpsimd", lambda g: g.partition_broadcast(shw_bc[:], sc_t[:, 4:5], channels=128),
                 reads=[buf("sc")], writes=[buf("shw")])
            for ic in range(8):
                dma("sync", WFF[:], wr["wfT"][ic], "WFF", "wf")
                B.op("vector", lambda v: v.tensor_scalar(
                    Q1[:], WFF[:], shw_bc[:], 224.0, OP.mult, OP.min),
                    reads=[buf("WFF"), buf("shw")], writes=[buf("Q1")])
                B.op("vector", lambda v, ic=ic: v.tensor_scalar(
                    WFQ[:, ic * D:(ic + 1) * D], Q1[:], -224.0, None, OP.max),
                    reads=[buf("Q1")], writes=[buf(("WFQ", ic))])

            B.op("gpsimd", lambda g: g.partition_all_reduce(
                amax_r[:], amax_r[:], channels=128, reduce_op=ReduceOp.max),
                reads=[buf("amax_r")], writes=[buf("amax_r")])
            dma("sync", ccin_h, amax_r[0:1, :], "ccin", "cc", extra_reads=["amax_r"])
            if _LOCAL_CC:
                B.op("gpsimd", lambda g: g.dma_start(out=ccout_h, in_=ccin_h),
                     reads=[buf("ccin")], writes=[buf("ccout")], dma="ccl")
            else:
                B.op("gpsimd", lambda g: g.collective_compute(
                    "AllReduce", OP.max, replica_groups=[[0, 1, 2, 3, 4, 5, 6, 7]],
                    ins=[ccin_h], outs=[ccout_h]),
                    reads=[buf("ccin")], writes=[buf("ccout")])
            dma("sync", amax_r[0:1, :], ccout_h, "amax_r", "cc", extra_reads=["ccout"])
            B.op("vector", lambda v: v.tensor_scalar(
                sc_t[:, 7:8], amax_r[0:1, :], 1e-12, None, OP.max),
                reads=[buf("amax_r"), buf("sc")], writes=[buf("sc")])
            B.op("vector", lambda v: v.reciprocal(sc_t[:, 6:7], sc_t[:, 7:8]),
                 reads=[buf("sc")], writes=[buf("sc")])
            B.op("vector", lambda v: v.tensor_scalar(
                sc_t[:, 0:1], sc_t[:, 6:7], 448.0, 448.0, OP.mult, OP.min),
                reads=[buf("sc")], writes=[buf("sc")])
            B.op("vector", lambda v: v.tensor_scalar(
                sc_t[:, 1:2], sc_t[:, 0:1], 0.5, None, OP.mult),
                reads=[buf("sc")], writes=[buf("sc")])
            B.op("vector", lambda v: v.reciprocal(sc_t[:, 2:3], sc_t[:, 0:1]),
                reads=[buf("sc")], writes=[buf("sc")])
            B.op("vector", lambda v: v.tensor_tensor(
                sc_t[:, 6:7], sc_t[:, 2:3], sc_t[:, 5:6], OP.mult),
                reads=[buf("sc")], writes=[buf("sc")])
            B.op("vector", lambda v: v.tensor_scalar(
                sc_t[:, 6:7], sc_t[:, 6:7], 4.0, None, OP.mult),
                reads=[buf("sc")], writes=[buf("sc")])
            B.op("gpsimd", lambda g: g.partition_broadcast(shx_bc[:], sc_t[:, 1:2], channels=128),
                 reads=[buf("sc")], writes=[buf("shx")])
            B.op("gpsimd", lambda g: g.partition_broadcast(d_bc[:], sc_t[:, 6:7], channels=128),
                 reads=[buf("sc")], writes=[buf("dbc")])

            def stageC(ch, k):
                xs = ch % 4
                qs = ch % 2
                xa = X2C[:, xs * D:(xs + 1) * D]
                o2 = O2R[:, qs * D:(qs + 1) * D]
                if k == 0:
                    dma("sync", xa, x2r[ch], ("x2c", xs), "xc%d" % xs,
                        extra_reads=[("x2w", ch)])
                    B.op("vector", lambda v, ch=ch, xa=xa: v.tensor_scalar(
                        SCR[:], xa, stats["r3"][:, ch:ch + 1],
                        stats["n3"][:, ch:ch + 1], OP.mult, OP.add),
                        reads=[buf(("x2c", xs)), buf(("r3", ch)), buf(("n3", ch))],
                        writes=[buf("SCR")])
                    B.group("tensor", [
                        (lambda tt, ic=ic: tt.transpose(
                            psT[:, ic * 128:(ic + 1) * 128],
                            SCR[:, ic * 128:(ic + 1) * 128], eye_f32[:]))
                        for ic in range(8)],
                        reads=[buf("SCR"), buf("eyef")], writes=[buf("psT")])
                    B.op("vector", lambda v: v.tensor_scalar(
                        Q1[:], psT[:], shx_bc[:], 224.0, OP.mult, OP.min),
                        reads=[buf("psT"), buf("shx")], writes=[buf("Q1")])
                    B.op("vector", lambda v, qs=qs: v.tensor_scalar(
                        XQT2[:, qs * D:(qs + 1) * D], Q1[:], -224.0, None, OP.max),
                        reads=[buf("Q1")], writes=[buf(("xqt", qs))])
                elif k == 1:
                    for oc2 in range(2):
                        B.group("tensor", [
                            (lambda tt, ic=ic, oc2=oc2, qs=qs: tt.matmul(
                                psF[2 * qs + oc2][:],
                                XQT2[:, qs * D + ic * 128: qs * D + ic * 128 + 128],
                                WFQ[:, ic * D + oc2 * 512: ic * D + oc2 * 512 + 512],
                                start=(ic == 0), stop=(ic == 7)))
                            for ic in range(8)],
                            reads=[buf(("xqt", qs))] + [buf(("WFQ", i)) for i in range(8)],
                            writes=[buf(("psF", 2 * qs + oc2))])
                elif k == 2:
                    for oc2 in range(2):
                        B.op("vector", lambda v, oc2=oc2, qs=qs, o2=o2:
                             v.scalar_tensor_tensor(
                                o2[:, oc2 * 512:(oc2 + 1) * 512],
                                psF[2 * qs + oc2][:], d_bc[:],
                                bbc[:, oc2 * 512:(oc2 + 1) * 512], OP.mult, OP.add),
                            reads=[buf(("psF", 2 * qs + oc2)), buf("dbc"), buf("bbc")],
                            writes=[buf(("o2r", qs))])
                    B.op("vector", lambda v, ch=ch, o2=o2: v.tensor_reduce(
                        stats["s1"][:, ch:ch + 1], o2, AX.X, OP.add),
                        reads=[buf(("o2r", qs))], writes=[buf(("s1", ch))])
                    B.op("vector", lambda v, ch=ch, o2=o2: v.scalar_tensor_tensor(
                        DMP[:], o2, 1.0, o2, OP.mult, OP.mult,
                        accum_out=stats["s2"][:, ch:ch + 1]),
                        reads=[buf(("o2r", qs))],
                        writes=[buf("DMP"), buf(("s2", ch))])
                    chain_dve(ch, "r4", "n4")
                    chain_act(ch, "r4")
                else:
                    chain_post(ch, "r4", "n4")
                    B.op("vector", lambda v, ch=ch, xa=xa: v.tensor_scalar(
                        xa, xa, stats["n4"][:, ch:ch + 1], None, OP.add),
                        reads=[buf(("x2c", xs)), buf(("n4", ch))],
                        writes=[buf(("x2c", xs))])
                    B.op("vector", lambda v, ch=ch, xa=xa, o2=o2: v.scalar_tensor_tensor(
                        xa, o2, stats["r4"][:, ch:ch + 1], xa, OP.mult, OP.add),
                        reads=[buf(("o2r", qs)), buf(("r4", ch)), buf(("x2c", xs))],
                        writes=[buf(("x2c", xs))])
                    dma("sync", yr[ch], xa, ("y", ch), "yw",
                        extra_reads=[("x2c", xs)])

            for step in range(NCH + 4):
                for k in (3, 2, 1, 0):
                    ch = step - k
                    if 0 <= ch < NCH:
                        stageC(ch, k)

        yq = B.dma_ctr("yw")
        x2q = B.dma_ctr("x2w")
        ytotal, x2total = yq.n, x2q.n

        def fn(sync):
            sync.wait_ge(yq.sem, ytotal)
            sync.wait_ge(x2q.sem, x2total)
        B.raw("sync", fn)

        block = st.enter_context(nc.Block())
        for ename, attr in (("sync", block.sync), ("vector", block.vector),
                            ("scalar", block.scalar), ("tensor", block.tensor),
                            ("gpsimd", block.gpsimd)):
            items = [fn for e, fn in B.script if e == ename]

            def run(eng, items=items):
                for fn in items:
                    fn(eng)
            attr(run)
    return nc


_NC_CACHE = {}


def _get_nc():
    if "nc" not in _NC_CACHE:
        nc = build_nc()
        lower_extended_insts(nc)
        _NC_CACHE["nc"] = nc
    return _NC_CACHE["nc"]


def kernel(L0_attn_query, L0_attn_mask, L0_attn_q_w, L0_attn_q_b, L0_attn_k_w,
           L0_attn_k_b, L0_attn_v_w, L0_attn_v_b, L0_attn_out_w, L0_attn_out_b,
           L0_ffn_weight, L0_ffn_bias):
    import ml_dtypes
    x = np.ascontiguousarray(np.asarray(L0_attn_query, dtype=np.float32))
    com = {
        "wqT": np.ascontiguousarray(np.asarray(L0_attn_q_w, np.float32).T),
        "wkT": np.ascontiguousarray(np.asarray(L0_attn_k_w, np.float32).T),
        "wvT": np.ascontiguousarray(np.asarray(L0_attn_v_w, np.float32).T),
        "woT": np.ascontiguousarray(np.asarray(L0_attn_out_w, np.float32).T),
        "wfT": np.ascontiguousarray(np.asarray(L0_ffn_weight, np.float32).T),
        "qb8": np.ascontiguousarray(np.asarray(L0_attn_q_b, np.float32).reshape(8, 128).T),
        "kb8": np.ascontiguousarray(np.asarray(L0_attn_k_b, np.float32).reshape(8, 128).T),
        "vbrow": np.asarray(L0_attn_v_b, np.float32).reshape(1, D),
        "obrow": np.asarray(L0_attn_out_b, np.float32).reshape(1, D),
        "fbrow": np.asarray(L0_ffn_bias, np.float32).reshape(1, D),
        "eye_bf": np.eye(128, dtype=np.float32).astype(ml_dtypes.bfloat16),
        "eye_f32": np.eye(128, dtype=np.float32),
        "ones32": np.ones((128, 64), dtype=np.float32).astype(ml_dtypes.bfloat16),
    }
    nc = _get_nc()
    in_maps = [dict(com, x=np.ascontiguousarray(x[b])) for b in range(8)]
    res = run_bass_kernel_spmd(nc, in_maps, core_ids=list(range(8)))
    out = np.stack([np.asarray(res.results[b]["y"]) for b in range(8)], axis=0)
    return out


# version 23
# speedup vs baseline: 1.0228x; 1.0228x over previous
import numpy as np
import concourse.bass as bass
import concourse.mybir as mybir
from concourse.bass_isa import ReduceOp
from concourse.library_config import attn as _LIB_ATTN
from concourse.library_overlay import lower_extended_insts
from concourse.bass_utils import run_bass_kernel_spmd

F32 = mybir.dt.float32
BF16 = mybir.dt.bfloat16
FP8 = mybir.dt.float8e4
AF = mybir.ActivationFunctionType
OP = mybir.AluOpType
AX = mybir.AxisListType

S = 4096
D = 1024
NCH = 32
NT = 8
EPS = 1e-5
ENGS = ("sync", "vector", "scalar", "tensor", "gpsimd")


class Buf:
    __slots__ = ("w", "r")

    def __init__(self):
        self.w = None
        self.r = []


class Ctr:
    def __init__(self, stack, nc, name, is_dma=False):
        self.sem = stack.enter_context(nc.semaphore(name))
        self.n = 0
        self.name = name
        self.is_dma = is_dma


class Builder:
    def __init__(self, nc, stack):
        self.nc = nc
        self.script = []
        self.ctr = {e: Ctr(stack, nc, "c_" + e) for e in ENGS}
        self.dctr = {}
        self.stack = stack
        self.waited = {e: {} for e in ENGS}

    def dma_ctr(self, name):
        if name not in self.dctr:
            self.dctr[name] = Ctr(self.stack, self.nc, "d_" + name, is_dma=True)
        return self.dctr[name]

    def _waits_for(self, eng, reads, writes):
        need = {}
        for b in reads:
            if b.w is not None:
                c, v = b.w
                need[c] = max(need.get(c, 0), v)
        for b in writes:
            if b.w is not None:
                c, v = b.w
                need[c] = max(need.get(c, 0), v)
            for c, v in b.r:
                need[c] = max(need.get(c, 0), v)
        out = []
        for c, v in need.items():
            if c.is_dma:
                v = c.n
            if self.waited[eng].get(c.name, 0) < v:
                out.append((c, v))
                self.waited[eng][c.name] = v
        return out

    def op(self, eng, emit, reads=(), writes=(), dma=None):
        waits = self._waits_for(eng, reads, writes)
        ctr = self.dma_ctr(dma) if dma else self.ctr[eng]
        inc = 16 if dma else 1
        ctr.n += inc
        val = ctr.n

        def fn(engine, waits=waits, emit=emit, ctr=ctr, inc=inc):
            for c, v in waits:
                engine.wait_ge(c.sem, v)
            emit(engine).then_inc(ctr.sem, inc)
        self.script.append((eng, fn))
        for b in writes:
            b.w = (ctr, val)
            b.r = []
        for b in reads:
            b.r.append((ctr, val))
        return (ctr, val)

    def group(self, eng, emits, reads=(), writes=()):
        waits = self._waits_for(eng, reads, writes)
        ctr = self.ctr[eng]
        ctr.n += 1
        val = ctr.n

        def fn(engine, waits=waits, emits=emits, ctr=ctr):
            for c, v in waits:
                engine.wait_ge(c.sem, v)
            ins = None
            for e in emits:
                ins = e(engine)
            ins.then_inc(ctr.sem, 1)
        self.script.append((eng, fn))
        for b in writes:
            b.w = (ctr, val)
            b.r = []
        for b in reads:
            b.r.append((ctr, val))
        return (ctr, val)

    def raw(self, eng, fn):
        self.script.append((eng, fn))

    def barrier(self):
        snap = [(c, c.n) for c in list(self.ctr.values()) + list(self.dctr.values())
                if c.n > 0]
        for e in ENGS:
            waits = [(c, v) for (c, v) in snap
                     if self.waited[e].get(c.name, 0) < v and c is not self.ctr[e]]
            for c, v in waits:
                self.waited[e][c.name] = v

            def fn(engine, waits=waits):
                for c, v in waits:
                    engine.wait_ge(c.sem, v)
            self.script.append((e, fn))


import os as _os
_LOCAL_CC = _os.environ.get("KERN_LOCAL_CC") == "1"


def build_nc():
    nc = bass.Bass("TRN2", target_bir_lowering=False)

    x_h = nc.dram_tensor("x", [S, D], F32, kind="ExternalInput").ap()
    w_h = {}
    for nm in ("wqT", "wkT", "wvT", "woT", "wfT"):
        w_h[nm] = nc.dram_tensor(nm, [D, D], F32, kind="ExternalInput").ap()
    qb_h = nc.dram_tensor("qb8", [128, 8], F32, kind="ExternalInput").ap()
    kb_h = nc.dram_tensor("kb8", [128, 8], F32, kind="ExternalInput").ap()
    vb_h = nc.dram_tensor("vbrow", [1, D], F32, kind="ExternalInput").ap()
    ob_h = nc.dram_tensor("obrow", [1, D], F32, kind="ExternalInput").ap()
    fb_h = nc.dram_tensor("fbrow", [1, D], F32, kind="ExternalInput").ap()
    eyeb_h = nc.dram_tensor("eye_bf", [128, 128], BF16, kind="ExternalInput").ap()
    eyef_h = nc.dram_tensor("eye_f32", [128, 128], F32, kind="ExternalInput").ap()
    ones_h = nc.dram_tensor("ones32", [128, 64], BF16, kind="ExternalInput").ap()
    y_h = nc.dram_tensor("y", [S, D], F32, kind="ExternalOutput").ap()
    x2_h = nc.dram_tensor("x2scr", [S, D], F32, kind="Internal").ap()
    wob_h = nc.dram_tensor("wobf", [8 * 128, D], BF16, kind="Internal").ap()
    wqb_h = nc.dram_tensor("wqbf", [8 * 128, D], BF16, kind="Internal").ap()
    xlt_h = nc.dram_tensor("xlt", [8 * 128, 8 * 512], BF16, kind="Internal").ap()
    ccin_h = nc.dram_tensor("ccin", [1, 1], F32, kind="Internal").ap()
    ccout_h = nc.dram_tensor("ccout", [1, 1], F32, kind="Internal").ap()

    xr = x_h.rearrange("(c p) d -> c p d", p=128)
    x2r = x2_h.rearrange("(c p) d -> c p d", p=128)
    yr = y_h.rearrange("(c p) d -> c p d", p=128)
    wr = {nm: w_h[nm].rearrange("(i p) o -> i p o", p=128) for nm in w_h}
    wob_r = wob_h.rearrange("(i p) o -> i p o", p=128)
    wqb_r = wqb_h.rearrange("(i p) o -> i p o", p=128)
    wqsl_w = wqb_h.rearrange("(oc p) (i q) -> i p oc q", p=128, q=128)
    xlt_r = xlt_h.rearrange("(i p) o -> i p o", p=128)

    from contextlib import ExitStack
    with ExitStack() as st:
        def sb(name, shape, dt=F32):
            return st.enter_context(nc.sbuf_tensor(name, shape, dt))

        B = Builder(nc, st)

        XFR = sb("XFR", [128, 2 * D], F32)
        stats = {}
        for nm in ("s1", "s2", "r1", "n1", "r2", "n2", "r3", "n3", "r4", "n4"):
            stats[nm] = sb("st_" + nm, [128, NCH], F32)
        qb8 = sb("qb8s", [128, 8], F32)
        kb8 = sb("kb8s", [128, 8], F32)
        bbc = sb("bbc", [128, D], F32)
        eye_bf = sb("eye_bf_s", [128, 128], BF16)
        ones32 = sb("ones32_s", [128, 64], BF16)
        amax_r = sb("amax_r", [128, 1], F32)
        amax_a = sb("amax_a", [128, 1], F32)
        wmax_r = sb("wmax_r", [128, 1], F32)
        sc_t = sb("sc_t", [1, 8], F32)
        shx_bc = sb("shx_bc", [128, 1], F32)
        d_bc = sb("d_bc", [128, 1], F32)

        from contextlib import ExitStack as _ES
        stAB = _ES()

        def sbA(name, shape, dt=F32):
            return stAB.enter_context(nc.sbuf_tensor(name, shape, dt))

        KT = sbA("KT", [128, 8 * S], BF16)
        V = sbA("V", [128, NCH * D], BF16)
        QT = sbA("QT", [128, 8 * 512], BF16)
        CTXT = sbA("CTXT", [128, 2 * 8 * 512], BF16)
        XLNT = sbA("XLNT", [128, 8 * 512], BF16)
        EXPT = sbA("EXPT", [128, 3 * 1024], BF16)
        XFP = sbA("XFP", [128, D], F32)
        WOS = sbA("WOS", [128, 2 * D], BF16)
        RBC = sbA("RBC", [128, 512], F32)

        bufs = {}

        def buf(key):
            if key not in bufs:
                bufs[key] = Buf()
            return bufs[key]

        def dma(eng_name, out_ap, in_ap, key, qname, extra_reads=()):
            B.op(eng_name, lambda e, o=out_ap, i=in_ap: e.dma_start(out=o, in_=i),
                 reads=[buf(k) for k in extra_reads], writes=[buf(key)], dma=qname)

        dma("sync", qb8[:], qb_h, "qb8", "pre")
        dma("sync", kb8[:], kb_h, "kb8", "pre")
        dma("sync", eye_bf[:], eyeb_h, "eyeb", "pre")
        dma("sync", ones32[:], ones_h, "ones", "pre")
        B.raw("gpsimd", lambda g: g.load_library(_LIB_ATTN))
        B.op("sync", lambda e: e.dma_start(out=XFP[0:1, :], in_=vb_h),
             reads=[buf("XFP")], writes=[buf("XFP")], dma="prow")
        B.op("gpsimd", lambda g: g.partition_broadcast(bbc[:], XFP[0:1, :], channels=128),
             reads=[buf("XFP")], writes=[buf("bbc")])
        B.op("vector", lambda v: v.memset(amax_r[:], 0.0), writes=[buf("amax_r")])
        B.op("vector", lambda v: v.memset(wmax_r[:], 0.0), writes=[buf("wmax_r")])

        def warm_burst(ps, key):
            B.group("tensor", [
                (lambda tt, i=i, ps=ps: tt.matmul(
                    ps[:, 0:128], eye_bf[:], eye_bf[:],
                    start=(i == 0), stop=(i == 31)))
                for i in range(32)],
                reads=[buf("eyeb")], writes=[buf(key)])

        def load_wbig(nm):
            for ic in range(8):
                dma("gpsimd", WBIG[:, ic * D:(ic + 1) * D], wr[nm][ic],
                    ("wbig", ic), "wq1")

        def wbig_read(ics):
            return [buf(("wbig", ic)) for ic in ics]

        def dump_ap(slot):
            return XFR[:, slot * D:(slot + 1) * D]

        def ln_stats(xap, col, xkey, dslot, dump=None, dkey=None):
            B.op("vector", lambda v, a=xap, c=col: v.tensor_reduce(
                stats["s1"][:, c:c + 1], a, AX.X, OP.add),
                reads=[buf(xkey)], writes=[buf(("s1", col))])
            if dump is None:
                dump, dkey = dump_ap(dslot), ("xfr", dslot)
            B.op("vector", lambda v, a=xap, c=col, dump=dump: v.scalar_tensor_tensor(
                dump, a, 1.0, a, OP.mult, OP.mult,
                accum_out=stats["s2"][:, c:c + 1]),
                reads=[buf(xkey)], writes=[buf(dkey), buf(("s2", col))])

        def ln_chain(c0, n, rk, nk):
            sl = slice(c0, c0 + n)
            sbufs = [buf(("s1", c)) for c in range(c0, c0 + n)] + \
                    [buf(("s2", c)) for c in range(c0, c0 + n)]
            nb = [buf((nk, c)) for c in range(c0, c0 + n)]
            rb = [buf((rk, c)) for c in range(c0, c0 + n)]
            tb = [buf(("s2", c)) for c in range(c0, c0 + n)]
            B.op("vector", lambda v: v.tensor_scalar(
                stats[nk][:, sl], stats["s1"][:, sl], 1.0 / D, None, OP.mult),
                reads=sbufs[:n], writes=nb)
            B.op("vector", lambda v: v.tensor_scalar(
                stats["s2"][:, sl], stats["s2"][:, sl], 1.0 / D, None, OP.mult),
                reads=sbufs[n:], writes=tb)
            B.op("vector", lambda v: v.tensor_tensor(
                stats[rk][:, sl], stats[nk][:, sl], stats[nk][:, sl], OP.mult),
                reads=nb, writes=rb)
            B.op("vector", lambda v: v.tensor_tensor(
                stats["s2"][:, sl], stats["s2"][:, sl], stats[rk][:, sl], OP.subtract),
                reads=tb + rb, writes=tb)
            B.op("vector", lambda v: v.tensor_scalar(
                stats["s2"][:, sl], stats["s2"][:, sl], EPS, None, OP.add),
                reads=tb, writes=tb)
            B.op("scalar", lambda s: s.activation(
                stats["s2"][:, sl], stats["s2"][:, sl], AF.Ln),
                reads=tb, writes=tb)
            B.op("vector", lambda v: v.tensor_scalar(
                stats["s2"][:, sl], stats["s2"][:, sl], -0.5, None, OP.mult),
                reads=tb, writes=tb)
            B.op("scalar", lambda s: s.activation(
                stats[rk][:, sl], stats["s2"][:, sl], AF.Exp),
                reads=tb, writes=rb)
            B.op("vector", lambda v: v.scalar_tensor_tensor(
                stats[nk][:, sl], stats[nk][:, sl], -1.0, stats[rk][:, sl],
                OP.mult, OP.mult),
                reads=nb + rb, writes=nb)

        def chain_dve(ch, rk, nk):
            sl = slice(ch, ch + 1)
            sb1 = [buf(("s1", ch))]
            sb2 = [buf(("s2", ch))]
            nb = [buf((nk, ch))]
            rb = [buf((rk, ch))]
            B.op("vector", lambda v: v.tensor_scalar(
                stats[nk][:, sl], stats["s1"][:, sl], 1.0 / D, None, OP.mult),
                reads=sb1, writes=nb)
            B.op("vector", lambda v: v.tensor_scalar(
                stats["s2"][:, sl], stats["s2"][:, sl], 1.0 / D, None, OP.mult),
                reads=sb2, writes=sb2)
            B.op("vector", lambda v: v.tensor_tensor(
                stats[rk][:, sl], stats[nk][:, sl], stats[nk][:, sl], OP.mult),
                reads=nb, writes=rb)
            B.op("vector", lambda v: v.tensor_tensor(
                stats["s2"][:, sl], stats["s2"][:, sl], stats[rk][:, sl], OP.subtract),
                reads=sb2 + rb, writes=sb2)
            B.op("vector", lambda v: v.tensor_scalar(
                stats["s2"][:, sl], stats["s2"][:, sl], EPS, None, OP.add),
                reads=sb2, writes=sb2)

        def chain_act(ch, rk):
            sl = slice(ch, ch + 1)
            B.op("scalar", lambda s: s.activation(
                stats["s2"][:, sl], stats["s2"][:, sl], AF.Ln),
                reads=[buf(("s2", ch))], writes=[buf(("s2", ch))])
            B.op("scalar", lambda s: s.activation(
                stats[rk][:, sl], stats["s2"][:, sl], AF.Exp, scale=-0.5),
                reads=[buf(("s2", ch))], writes=[buf((rk, ch))])

        def chain_post(ch, rk, nk):
            sl = slice(ch, ch + 1)
            B.op("vector", lambda v: v.scalar_tensor_tensor(
                stats[nk][:, sl], stats[nk][:, sl], -1.0, stats[rk][:, sl],
                OP.mult, OP.mult),
                reads=[buf((nk, ch)), buf((rk, ch))], writes=[buf((nk, ch))])

        def apply_transpose(ch, ps_tr, ps_key, xa=None, xkey="XFP"):
            if xa is None:
                xa = XFP[:]
            j4 = ch % 4
            B.op("vector", lambda v, c=ch, xa=xa: v.tensor_scalar(
                EXPT[:, 0:D], xa, stats["r1"][:, c:c + 1],
                stats["n1"][:, c:c + 1], OP.mult, OP.add),
                reads=[buf(xkey), buf(("r1", ch)), buf(("n1", ch))],
                writes=[buf(("expt", 0))])
            B.group("tensor", [
                (lambda t, ic=ic, ps_tr=ps_tr: t.transpose(
                    ps_tr[:, ic * 128:(ic + 1) * 128],
                    EXPT[:, ic * 128:(ic + 1) * 128], eye_bf[:]))
                for ic in range(8)],
                reads=[buf(("expt", 0)), buf("eyeb")],
                writes=[buf(ps_key)])
            xv = XLNT[:, :].rearrange("p (ic q) -> p ic q", ic=8)[:, :, j4 * 128:(j4 + 1) * 128]
            pv = ps_tr[:, :].rearrange("p (ic q) -> p ic q", ic=8)
            B.op("vector", lambda v, xv=xv, pv=pv: v.tensor_copy(xv, pv),
                 reads=[buf(ps_key)], writes=[buf(("xlnt", j4))])

        with ExitStack() as stA:
            ps_trA = stA.enter_context(nc.psum_tensor("ps_trA", [128, 1024], BF16))
            ps_kv0 = stA.enter_context(nc.psum_tensor("ps_kv0", [128, 512], F32))
            ps_kv1 = stA.enter_context(nc.psum_tensor("ps_kv1", [128, 512], F32))
            ps_kv = [ps_kv0, ps_kv1]
            WBIG = stA.enter_context(nc.sbuf_tensor("WBIG", [128, 8 * D], BF16))

            load_wbig("wkT")
            warm_burst(ps_kv0, ("ps_kv", 0))

            for nm, k in (("woT", "wob"), ("wqT", "wqb")):
                for ic in range(8):
                    slot = ic % 2
                    vst = V[:, slot * D:(slot + 1) * D]
                    B.op("gpsimd", lambda g, vst=vst, nm=nm, ic=ic: g.dma_start(
                        out=vst, in_=wr[nm][ic]),
                        reads=[], writes=[buf(("vpc", slot))], dma="pc%d" % slot)
                    if k == "wob":
                        for oc2 in range(2):
                            blk = wob_r[oc2 * 4 + ic // 2]
                            dma("sync", blk[:, (ic % 2) * 512:(ic % 2) * 512 + 512],
                                vst[:, oc2 * 512:(oc2 + 1) * 512],
                                ("wob", ic), "pcw", extra_reads=[("vpc", slot)])
                    else:
                        dma("sync", wqsl_w[ic],
                            vst.rearrange("p (oc q) -> p oc q", q=128),
                            ("wqb", ic), "pcw", extra_reads=[("vpc", slot)])

            def xslot(ch):
                if ch % 2 == 0:
                    return (XFP[:], "XFP", "xq0")
                return (XFR[:, 0:D], ("xfr", 0), "xq1")

            def stats_block(s_i):
                for j4 in range(4):
                    ch = 4 * s_i + j4
                    xa, xkey, xqn = xslot(ch)
                    dma("sync", xa, xr[ch], xkey, xqn)
                    ln_stats(xa, ch, xkey, (ch + 1) % 2)
                ln_chain(4 * s_i, 4, "r1", "n1")

            def apply_block(s_i):
                for j4 in range(4):
                    ch = 4 * s_i + j4
                    xa, xkey, xqn = xslot(ch)
                    dma("sync", xa, xr[ch], xkey, xqn)
                    apply_transpose(ch, ps_trA, "ps_trA", xa, xkey)
                B.op("sync", lambda e, s_i=s_i: e.dma_start(
                    out=xlt_r[s_i], in_=XLNT[:]),
                    reads=[buf(("xlnt", j)) for j in range(4)],
                    writes=[buf(("xlt", s_i))], dma="xlw")

            def kproj_block(s_i):
                for oc in range(8):
                    pk = ps_kv[oc % 2]
                    B.group("tensor", [
                        (lambda t, ic=ic, pk=pk, oc=oc: t.matmul(
                            pk[:], WBIG[:, ic * D + oc * 128: ic * D + oc * 128 + 128],
                            XLNT[:, ic * 512:(ic + 1) * 512],
                            start=(ic == 0), stop=(ic == 7)))
                        for ic in range(8)],
                        reads=wbig_read(range(8)) + [buf(("xlnt", j)) for j in range(4)],
                        writes=[buf(("ps_kv", oc % 2))])
                    B.op("vector", lambda v, oc=oc, s_i=s_i, pk=pk: v.tensor_scalar(
                        KT[:, oc * S + s_i * 512: oc * S + s_i * 512 + 512],
                        pk[:], kb8[:, oc:oc + 1], None, OP.add),
                        reads=[buf(("ps_kv", oc % 2)), buf("kb8")],
                        writes=[buf(("KT", oc, s_i))])

            for s_i in range(NT):
                stats_block(s_i)
                apply_block(s_i)
                kproj_block(s_i)

            load_wbig("wvT")
            for s_i in range(NT):
                B.op("sync", lambda e, s_i=s_i: e.dma_start(
                    out=XLNT[:], in_=xlt_r[s_i]),
                    reads=[buf(("xlt", s_i))],
                    writes=[buf(("xlnt", j)) for j in range(4)], dma="xlr")
                for ov in range(2):
                    for tc in range(4):
                        pk = ps_kv[tc % 2]
                        B.group("tensor", [
                            (lambda t, ic=ic, tc=tc, pk=pk, ov=ov: t.matmul(
                                pk[:],
                                XLNT[:, ic * 512 + tc * 128: ic * 512 + (tc + 1) * 128],
                                WBIG[:, ic * D + ov * 512: ic * D + ov * 512 + 512],
                                start=(ic == 0), stop=(ic == 7)))
                            for ic in range(8)],
                            reads=wbig_read(range(8)) + [buf(("xlnt", tc))],
                            writes=[buf(("ps_kv", tc % 2))])
                        ch = 4 * s_i + tc
                        vw = [buf(("V", ch, ov))]
                        if ch < 2:
                            vw.append(buf(("vpc", ch)))
                        B.op("vector", lambda v, ch=ch, ov=ov, pk=pk: v.tensor_tensor(
                            V[:, ch * D + ov * 512: ch * D + ov * 512 + 512],
                            pk[:], bbc[:, ov * 512:(ov + 1) * 512], OP.add),
                            reads=[buf(("ps_kv", tc % 2)), buf("bbc")],
                            writes=vw)

        for ic in range(8):
            fsl = XFP[:] if ic % 2 == 0 else XFR[:, 0:D]
            fkey = "XFP" if ic % 2 == 0 else ("xfr", 0)
            dma("sync", fsl, wr["wfT"][ic], fkey, "wf%d" % (ic % 2))
            B.op("vector", lambda v, fsl=fsl: v.tensor_reduce(
                amax_a[:], fsl, AX.X, OP.max, apply_absolute_value=True),
                reads=[buf(fkey)], writes=[buf("amax_a")])
            B.op("vector", lambda v: v.tensor_tensor(
                wmax_r[:], wmax_r[:], amax_a[:], OP.max),
                reads=[buf("wmax_r"), buf("amax_a")], writes=[buf("wmax_r")])
        B.op("gpsimd", lambda g: g.partition_all_reduce(
            wmax_r[:], wmax_r[:], channels=128, reduce_op=ReduceOp.max),
            reads=[buf("wmax_r")], writes=[buf("wmax_r")])

        onec = sbA("onec", [1, 128], F32)
        WQS = sbA("WQS", [128, D], BF16)
        B.op("vector", lambda v: v.memset(onec[:], 1.0), writes=[buf("onec")])
        B.barrier()
        B.op("sync", lambda e: e.dma_start(out=bbc[0:1, :], in_=ob_h),
             reads=[buf("bbc")], writes=[buf("bbc")], dma="prow")

        with ExitStack() as stB:
            ps_sc = stB.enter_context(nc.psum_tensor("ps_sc", [128, 2048], F32))
            ps_ctx = [stB.enter_context(nc.psum_tensor(f"ps_ctx{i}", [128, 512], F32))
                      for i in range(2)]
            ps_sum = stB.enter_context(nc.psum_tensor("ps_sum", [128, 512], F32))
            ps_op = stB.enter_context(nc.psum_tensor("ps_op", [128, 512], F32))

            def prep_x(t):
                B.op("sync", lambda e, t=t: e.dma_start(out=XLNT[:], in_=xlt_r[t]),
                     reads=[buf(("xlt", t))],
                     writes=[buf(("xlnt", j)) for j in range(4)], dma="xlr")

            def prep_oc_steps(t, oc, ps, pkey):
                def load():
                    dma("sync", WQS[:], wqb_r[oc], "WQS", "wqsl",
                        extra_reads=[("wqb", i) for i in range(8)])

                def mms():
                    B.group("tensor", [
                        (lambda tt, ic=ic, ps=ps: tt.matmul(
                            ps[:], WQS[:, ic * 128:(ic + 1) * 128],
                            XLNT[:, ic * 512:(ic + 1) * 512],
                            start=(ic == 0), stop=(ic == 7)))
                        for ic in range(8)],
                        reads=[buf("WQS")] + [buf(("xlnt", j)) for j in range(4)],
                        writes=[buf(pkey)])

                def evac():
                    B.op("vector", lambda v, oc=oc, ps=ps: v.tensor_scalar(
                        QT[:, oc * 512:(oc + 1) * 512], ps[:],
                        qb8[:, oc:oc + 1], 0.125, OP.add, OP.mult),
                        reads=[buf(pkey), buf("qb8")],
                        writes=[buf(("QT", oc))])
                return [load, mms, evac]

            def prep_oc(t, oc, ps, pkey):
                for s in prep_oc_steps(t, oc, ps, pkey):
                    s()

            def prep_mm(t):
                for oc in range(8):
                    prep_oc(t, oc, ps_ctx[oc % 2], ("ps_ctx", oc % 2))

            def fill_pass_steps(tp, li, oc2):
                ch = 4 * tp + li
                half = tp % 2
                rslot = ch % 2
                xa = XFR[:, rslot * D:(rslot + 1) * D]
                steps = []
                if oc2 == 0:
                    steps.append(lambda: dma(
                        "sync", xa, xr[ch], ("xfr", rslot), "xr%d" % rslot))

                def mk_load(pc2):
                    ws = pc2 % 2
                    def em():
                        dma("sync", WOS[:, ws * D:(ws + 1) * D],
                            wob_r[oc2 * 4 + pc2], ("wos", ws), "wosf%d" % ws,
                            extra_reads=[("wob", 2 * pc2), ("wob", 2 * pc2 + 1)])
                    return em

                def mk_mms(pc2):
                    ws = pc2 % 2

                    def emit():
                        for h2 in range(2):
                            dc = 2 * pc2 + h2
                            B.group("tensor", [
                                (lambda tt, dc=dc, h2=h2, ws=ws: tt.matmul(
                                    ps_op[:],
                                    CTXT[:, half * 4096 + dc * 512 + li * 128:
                                         half * 4096 + dc * 512 + li * 128 + 128],
                                    WOS[:, ws * D + h2 * 512: ws * D + h2 * 512 + 512],
                                    start=(dc == 0), stop=False))],
                                reads=[buf(("wos", ws)), buf(("ctxt", half, dc))],
                                writes=[buf("ps_op")])
                    return emit

                steps.append(mk_load(0))
                for pc2 in range(3):
                    steps.append(lambda pc2=pc2: (mk_mms(pc2)(), mk_load(pc2 + 1)()))

                def last_mms():
                    mk_mms(3)()
                    B.group("tensor", [
                        (lambda tt: tt.matmul(
                            ps_op[:], onec[:], bbc[0:1, oc2 * 512:(oc2 + 1) * 512],
                            start=False, stop=True))],
                        reads=[buf("onec"), buf("bbc")], writes=[buf("ps_op")])
                steps.append(last_mms)

                def copy_stats():
                    xh = XFP[:, oc2 * 512:(oc2 + 1) * 512]
                    B.op("vector", lambda v, xh=xh: v.tensor_copy(xh, ps_op[:]),
                         reads=[buf("ps_op")], writes=[buf("XFP")])
                    if oc2 == 0:
                        B.op("vector", lambda v, ch=ch: v.tensor_reduce(
                            stats["s1"][:, ch:ch + 1], ps_op[:], AX.X, OP.add),
                            reads=[buf("ps_op")], writes=[buf(("s1", ch))])
                        B.op("vector", lambda v, ch=ch, xh=xh:
                             v.scalar_tensor_tensor(
                                WOS[:, 0:512], ps_op[:], 1.0, xh, OP.mult, OP.mult,
                                accum_out=stats["s2"][:, ch:ch + 1]),
                            reads=[buf("ps_op"), buf("XFP")],
                            writes=[buf(("wos", 0)), buf(("s2", ch))])
                    else:
                        B.op("vector", lambda v: v.tensor_reduce(
                            amax_a[:], ps_op[:], AX.X, OP.add),
                            reads=[buf("ps_op")], writes=[buf("amax_a")])
                        B.op("vector", lambda v, ch=ch: v.tensor_tensor(
                            stats["s1"][:, ch:ch + 1], stats["s1"][:, ch:ch + 1],
                            amax_a[:], OP.add),
                            reads=[buf(("s1", ch)), buf("amax_a")],
                            writes=[buf(("s1", ch))])
                        B.op("vector", lambda v, ch=ch, xh=xh:
                             v.scalar_tensor_tensor(
                                WOS[:, 0:512], ps_op[:], 1.0, xh, OP.mult, OP.mult,
                                accum_out=stats["n4"][:, ch:ch + 1]),
                            reads=[buf("ps_op"), buf("XFP")],
                            writes=[buf(("wos", 0)), buf(("n4", ch))])
                        B.op("vector", lambda v, ch=ch: v.tensor_tensor(
                            stats["s2"][:, ch:ch + 1], stats["s2"][:, ch:ch + 1],
                            stats["n4"][:, ch:ch + 1], OP.add),
                            reads=[buf(("s2", ch)), buf(("n4", ch))],
                            writes=[buf(("s2", ch))])
                steps.append(copy_stats)
                return steps

            def fill_k1_chain(tp, li):
                ch = 4 * tp + li
                chain_dve(ch, "r2", "n2")
                chain_act(ch, "r2")

            def fill_k2(tp, li):
                ch = 4 * tp + li
                rslot = ch % 2
                xa = XFR[:, rslot * D:(rslot + 1) * D]
                chain_post(ch, "r2", "n2")
                B.op("vector", lambda v, ch=ch, xa=xa: v.tensor_scalar(
                    xa, xa, stats["n2"][:, ch:ch + 1], None, OP.add),
                    reads=[buf(("xfr", rslot)), buf(("n2", ch))],
                    writes=[buf(("xfr", rslot))])
                for oc2 in range(2):
                    B.op("vector", lambda v, ch=ch, xa=xa, oc2=oc2:
                         v.scalar_tensor_tensor(
                            xa[:, oc2 * 512:(oc2 + 1) * 512],
                            XFP[:, oc2 * 512:(oc2 + 1) * 512],
                            stats["r2"][:, ch:ch + 1],
                            xa[:, oc2 * 512:(oc2 + 1) * 512],
                            OP.mult, OP.add),
                        reads=[buf("XFP"), buf(("r2", ch)), buf(("xfr", rslot))],
                        writes=[buf(("xfr", rslot))])
                dma("sync", x2r[ch], xa, ("x2w", ch), "x2w",
                    extra_reads=[("xfr", rslot)])
                B.op("vector", lambda v, ch=ch, xa=xa: v.tensor_reduce(
                    stats["s1"][:, ch:ch + 1], xa, AX.X, OP.add),
                    reads=[buf(("xfr", rslot))], writes=[buf(("s1", ch))])
                B.op("vector", lambda v, ch=ch, xa=xa: v.scalar_tensor_tensor(
                    WOS[:, 0:D], xa, 1.0, xa, OP.mult, OP.mult,
                    accum_out=stats["s2"][:, ch:ch + 1]),
                    reads=[buf(("xfr", rslot))],
                    writes=[buf(("wos", 0)), buf(("s2", ch))])
                chain_dve(ch, "r3", "n3")
                chain_act(ch, "r3")

            def fill_k3(tp, li):
                ch = 4 * tp + li
                rslot = ch % 2
                xa = XFR[:, rslot * D:(rslot + 1) * D]
                chain_post(ch, "r3", "n3")
                B.op("vector", lambda v, ch=ch, xa=xa: v.tensor_reduce(
                    stats["r4"][:, ch:ch + 1], xa, AX.X, OP.max),
                    reads=[buf(("xfr", rslot))], writes=[buf(("r4", ch))])
                B.op("vector", lambda v, ch=ch, xa=xa: v.tensor_reduce(
                    stats["n4"][:, ch:ch + 1], xa, AX.X, OP.min),
                    reads=[buf(("xfr", rslot))], writes=[buf(("n4", ch))])
                B.op("vector", lambda v, ch=ch: v.tensor_scalar(
                    stats["r4"][:, ch:ch + 1], stats["r4"][:, ch:ch + 1],
                    stats["r3"][:, ch:ch + 1], stats["n3"][:, ch:ch + 1],
                    OP.mult, OP.add),
                    reads=[buf(("r4", ch)), buf(("r3", ch)), buf(("n3", ch))],
                    writes=[buf(("r4", ch))])
                B.op("vector", lambda v, ch=ch: v.tensor_scalar(
                    stats["n4"][:, ch:ch + 1], stats["n4"][:, ch:ch + 1],
                    stats["r3"][:, ch:ch + 1], stats["n3"][:, ch:ch + 1],
                    OP.mult, OP.add),
                    reads=[buf(("n4", ch)), buf(("r3", ch)), buf(("n3", ch))],
                    writes=[buf(("n4", ch))])
                B.op("vector", lambda v, ch=ch: v.tensor_scalar(
                    stats["n4"][:, ch:ch + 1], stats["n4"][:, ch:ch + 1],
                    -1.0, None, OP.mult),
                    reads=[buf(("n4", ch))], writes=[buf(("n4", ch))])
                B.op("vector", lambda v, ch=ch: v.tensor_tensor(
                    amax_a[:], stats["r4"][:, ch:ch + 1],
                    stats["n4"][:, ch:ch + 1], OP.max),
                    reads=[buf(("r4", ch)), buf(("n4", ch))],
                    writes=[buf("amax_a")])
                B.op("vector", lambda v: v.tensor_tensor(
                    amax_r[:], amax_r[:], amax_a[:], OP.max),
                    reads=[buf("amax_r"), buf("amax_a")], writes=[buf("amax_r")])

            def fill_stage(tp, li, k):
                if k <= 1:
                    for s in fill_pass_steps(tp, li, k):
                        s()
                    if k == 1:
                        fill_k1_chain(tp, li)
                elif k == 2:
                    fill_k2(tp, li)
                else:
                    fill_k3(tp, li)

            def fill_at(tp, c):
                for li in range(4):
                    k = c - 2 * li
                    if 0 <= k <= 3:
                        pass
                for li, k in sorted(
                        [(li, c - 2 * li) for li in range(4)
                         if 0 <= c - 2 * li <= 3],
                        key=lambda x: -x[1]):
                    fill_stage(tp, li, k)

            prep_x(0)
            prep_mm(0)
            warm_burst(ps_op, "ps_op")
            def scores_emits_f(kk, c):
                slot = kk % 2
                return [
                    (lambda tt, kk=kk, c=c, slot=slot: tt.matmul(
                        ps_sc[:, slot * 1024: slot * 1024 + 512],
                        KT[0:64, c * S + kk * 128: c * S + kk * 128 + 128],
                        QT[0:64, c * 512:(c + 1) * 512],
                        start=True, stop=True, tile_position=(0, 0))),
                    (lambda tt, kk=kk, c=c, slot=slot: tt.matmul(
                        ps_sc[:, slot * 1024 + 512: slot * 1024 + 1024],
                        KT[64:128, c * S + kk * 128: c * S + kk * 128 + 128],
                        QT[64:128, c * 512:(c + 1) * 512],
                        start=True, stop=True, tile_position=(64, 0)))]

            def exp_op_f(kk):
                slot = kk % 2
                es = kk % 3
                B.op("scalar", lambda s, slot=slot, es=es: s.activation(
                    EXPT[:, es * 1024:(es + 1) * 1024],
                    ps_sc[:, slot * 1024:(slot + 1) * 1024], AF.Exp),
                    reads=[buf(("ps_sc", 2 * slot)), buf(("ps_sc", 2 * slot + 1))],
                    writes=[buf(("expt", es))])

            def head(t, c):
                for kk in range(3):
                    B.group("tensor", scores_emits_f(kk, c),
                            reads=[buf(("KT", c, 0)), buf(("QT", c))],
                            writes=[buf(("ps_sc", 2 * (kk % 2))),
                                    buf(("ps_sc", 2 * (kk % 2) + 1))])
                    exp_op_f(kk)

            for t in range(NT):
                for c in range(8):
                    pc = ps_ctx[c % 2]
                    cb = buf(("ps_ctx", c % 2))

                    def scores_emits(kk, c=c):
                        return scores_emits_f(kk, c)

                    def exp_op(kk):
                        exp_op_f(kk)

                    if t == 0 and c == 0:
                        head(0, 0)
                    if t + 1 < NT and c == 0:
                        prep_x(t + 1)
                    fillers = []
                    prep_evac = None
                    if t > 0:
                        fillers += fill_pass_steps(t - 1, c // 2, c % 2)
                    if t + 1 < NT:
                        psteps = prep_oc_steps(t + 1, c, ps_op, "ps_op")
                        fillers += psteps[:2]
                        prep_evac = psteps[2]
                    for kk in range(NCH):
                        slot = kk % 2
                        eb = buf(("expt", kk % 3))
                        if kk == 0 and t > 0 and c >= 2 and c % 2 == 0:
                            fill_k2(t - 1, (c - 2) // 2)
                        if kk >= 6 and kk % 2 == 0 and (kk - 6) // 2 < len(fillers):
                            fillers[(kk - 6) // 2]()
                        if kk + 3 < NCH:
                            B.group("tensor", scores_emits(kk + 3),
                                    reads=[buf(("KT", c, (kk + 3) // 4)), buf(("QT", c))],
                                    writes=[buf(("ps_sc", 2 * ((kk + 3) % 2))),
                                            buf(("ps_sc", 2 * ((kk + 3) % 2) + 1))])
                        es = kk % 3
                        emits = [
                            (lambda tt, kk=kk, c=c, es=es, pc=pc: tt.matmul(
                                pc[0:64, :],
                                V[:, kk * D + c * 128: kk * D + c * 128 + 64],
                                EXPT[:, es * 1024: es * 1024 + 512],
                                start=(kk == 0), stop=(kk == NCH - 1),
                                tile_position=(0, 0), skip_group_check=True)),
                            (lambda tt, kk=kk, c=c, es=es, pc=pc: tt.matmul(
                                pc[64:128, :],
                                V[:, kk * D + c * 128 + 64: kk * D + c * 128 + 128],
                                EXPT[:, es * 1024 + 512: es * 1024 + 1024],
                                start=(kk == 0), stop=(kk == NCH - 1),
                                tile_position=(0, 64), skip_group_check=True)),
                            (lambda tt, kk=kk, es=es: tt.matmul(
                                ps_sum[0:64, :], ones32[:],
                                EXPT[:, es * 1024: es * 1024 + 512],
                                start=(kk == 0), stop=(kk == NCH - 1),
                                tile_position=(0, 0), skip_group_check=True)),
                            (lambda tt, kk=kk, es=es: tt.matmul(
                                ps_sum[64:128, :], ones32[:],
                                EXPT[:, es * 1024 + 512: es * 1024 + 1024],
                                start=(kk == 0), stop=(kk == NCH - 1),
                                tile_position=(0, 64), skip_group_check=True))]
                        rd = [eb, buf(("V", kk, 0)), buf(("V", kk, 1)), buf("ones")]
                        wrt = [cb, buf("ps_sum")]
                        B.group("tensor", emits, reads=rd, writes=wrt)
                        if kk + 3 < NCH:
                            exp_op(kk + 3)
                    B.op("vector", lambda v: v.tensor_copy(RBC[:], ps_sum[:]),
                         reads=[buf("ps_sum")], writes=[buf("RBC")])
                    B.op("vector", lambda v: v.reciprocal_approx_fast(RBC[:], RBC[:]),
                         reads=[buf("RBC")], writes=[buf("RBC")])
                    B.op("vector", lambda v, c=c, pc=pc, t=t: v.tensor_tensor(
                        CTXT[0:64, (t % 2) * 4096 + c * 512:
                             (t % 2) * 4096 + (c + 1) * 512], pc[0:64, :],
                        RBC[0:64, :], OP.mult),
                        reads=[cb, buf("RBC")], writes=[buf(("ctxt", t % 2, c))])
                    B.op("vector", lambda v, c=c, pc=pc, t=t: v.tensor_tensor(
                        CTXT[64:128, (t % 2) * 4096 + c * 512:
                             (t % 2) * 4096 + (c + 1) * 512], pc[64:128, :],
                        RBC[64:128, :], OP.mult),
                        reads=[cb, buf("RBC")], writes=[buf(("ctxt", t % 2, c))])
                    if c + 1 < 8:
                        head(t, c + 1)
                    elif t + 1 < NT:
                        head(t + 1, 0)
                    if prep_evac is not None:
                        prep_evac()
                    if t > 0 and c % 2 == 1:
                        fill_k1_chain(t - 1, c // 2)
                        if c >= 3:
                            fill_k3(t - 1, (c - 3) // 2)
                if t > 0:
                    fill_stage(t - 1, 3, 2)
                    fill_stage(t - 1, 3, 3)
            for stp in range(11):
                for li in range(4):
                    k = stp - 2 * li
                    if 0 <= k <= 3:
                        fill_stage(NT - 1, li, k)

        B.barrier()
        B.op("sync", lambda e: e.dma_start(out=XFP[0:1, :], in_=fb_h),
             reads=[buf("XFP")], writes=[buf("XFP")], dma="prow")
        B.op("gpsimd", lambda g: g.partition_broadcast(bbc[:], XFP[0:1, :], channels=128),
             reads=[buf("XFP"), buf("bbc")], writes=[buf("bbc")])
        stAB.close()
        SCR = sb("SCRC", [128, D], F32)
        eye_f32 = sb("eye_f32_s", [128, 128], F32)
        dma("sync", eye_f32[:], eyef_h, "eyef", "eyefq")
        WFQ = sb("WFQ", [128, 8 * D], FP8)
        WFF = sb("WFF", [128, D], F32)
        Q1 = sb("Q1", [128, D], F32)
        XQT2 = sb("XQT2", [128, 2 * D], FP8)
        O2R = sb("O2R", [128, 2 * D], BF16)
        X2C = sb("X2C", [128, 4 * D], F32)
        DMP = sb("DMP", [128, D], F32)
        shw_bc = sb("shw_bc", [128, 1], F32)

        with ExitStack() as stC:
            psT = stC.enter_context(nc.psum_tensor("psT", [128, 1024], F32))
            psF = [stC.enter_context(nc.psum_tensor(f"psF{i}", [128, 512], F32))
                   for i in range(4)]
            warm_burst(psF[0], ("psF", 0))

            B.op("vector", lambda v: v.memset(sc_t[:], 0.0), writes=[buf("sc")])
            B.op("vector", lambda v: v.tensor_scalar(
                sc_t[:, 7:8], wmax_r[0:1, :], 1e-12, None, OP.max),
                reads=[buf("wmax_r"), buf("sc")], writes=[buf("sc")])
            B.op("vector", lambda v: v.reciprocal(sc_t[:, 6:7], sc_t[:, 7:8]),
                 reads=[buf("sc")], writes=[buf("sc")])
            B.op("vector", lambda v: v.tensor_scalar(
                sc_t[:, 3:4], sc_t[:, 6:7], 448.0, 448.0, OP.mult, OP.min),
                reads=[buf("sc")], writes=[buf("sc")])
            B.op("vector", lambda v: v.tensor_scalar(
                sc_t[:, 4:5], sc_t[:, 3:4], 0.5, None, OP.mult),
                reads=[buf("sc")], writes=[buf("sc")])
            B.op("vector", lambda v: v.reciprocal(sc_t[:, 5:6], sc_t[:, 3:4]),
                 reads=[buf("sc")], writes=[buf("sc")])
            B.op("gpsimd", lambda g: g.partition_broadcast(shw_bc[:], sc_t[:, 4:5], channels=128),
                 reads=[buf("sc")], writes=[buf("shw")])
            for ic in range(8):
                dma("sync", WFF[:], wr["wfT"][ic], "WFF", "wf")
                B.op("vector", lambda v: v.tensor_scalar(
                    Q1[:], WFF[:], shw_bc[:], 224.0, OP.mult, OP.min),
                    reads=[buf("WFF"), buf("shw")], writes=[buf("Q1")])
                B.op("vector", lambda v, ic=ic: v.tensor_scalar(
                    WFQ[:, ic * D:(ic + 1) * D], Q1[:], -224.0, None, OP.max),
                    reads=[buf("Q1")], writes=[buf(("WFQ", ic))])

            B.op("gpsimd", lambda g: g.partition_all_reduce(
                amax_r[:], amax_r[:], channels=128, reduce_op=ReduceOp.max),
                reads=[buf("amax_r")], writes=[buf("amax_r")])
            dma("sync", ccin_h, amax_r[0:1, :], "ccin", "cc", extra_reads=["amax_r"])
            if _LOCAL_CC:
                B.op("gpsimd", lambda g: g.dma_start(out=ccout_h, in_=ccin_h),
                     reads=[buf("ccin")], writes=[buf("ccout")], dma="ccl")
            else:
                B.op("gpsimd", lambda g: g.collective_compute(
                    "AllReduce", OP.max, replica_groups=[[0, 1, 2, 3, 4, 5, 6, 7]],
                    ins=[ccin_h], outs=[ccout_h]),
                    reads=[buf("ccin")], writes=[buf("ccout")])
            dma("sync", amax_r[0:1, :], ccout_h, "amax_r", "cc", extra_reads=["ccout"])
            B.op("vector", lambda v: v.tensor_scalar(
                sc_t[:, 7:8], amax_r[0:1, :], 1e-12, None, OP.max),
                reads=[buf("amax_r"), buf("sc")], writes=[buf("sc")])
            B.op("vector", lambda v: v.reciprocal(sc_t[:, 6:7], sc_t[:, 7:8]),
                 reads=[buf("sc")], writes=[buf("sc")])
            B.op("vector", lambda v: v.tensor_scalar(
                sc_t[:, 0:1], sc_t[:, 6:7], 448.0, 448.0, OP.mult, OP.min),
                reads=[buf("sc")], writes=[buf("sc")])
            B.op("vector", lambda v: v.tensor_scalar(
                sc_t[:, 1:2], sc_t[:, 0:1], 0.5, None, OP.mult),
                reads=[buf("sc")], writes=[buf("sc")])
            B.op("vector", lambda v: v.reciprocal(sc_t[:, 2:3], sc_t[:, 0:1]),
                reads=[buf("sc")], writes=[buf("sc")])
            B.op("vector", lambda v: v.tensor_tensor(
                sc_t[:, 6:7], sc_t[:, 2:3], sc_t[:, 5:6], OP.mult),
                reads=[buf("sc")], writes=[buf("sc")])
            B.op("vector", lambda v: v.tensor_scalar(
                sc_t[:, 6:7], sc_t[:, 6:7], 4.0, None, OP.mult),
                reads=[buf("sc")], writes=[buf("sc")])
            B.op("gpsimd", lambda g: g.partition_broadcast(shx_bc[:], sc_t[:, 1:2], channels=128),
                 reads=[buf("sc")], writes=[buf("shx")])
            B.op("gpsimd", lambda g: g.partition_broadcast(d_bc[:], sc_t[:, 6:7], channels=128),
                 reads=[buf("sc")], writes=[buf("dbc")])

            def stageC(ch, k):
                xs = ch % 4
                qs = ch % 2
                xa = X2C[:, xs * D:(xs + 1) * D]
                o2 = O2R[:, qs * D:(qs + 1) * D]
                if k == 0:
                    dma("sync", xa, x2r[ch], ("x2c", xs), "xc%d" % xs,
                        extra_reads=[("x2w", ch)])
                    B.op("vector", lambda v, ch=ch, xa=xa: v.tensor_scalar(
                        SCR[:], xa, stats["r3"][:, ch:ch + 1],
                        stats["n3"][:, ch:ch + 1], OP.mult, OP.add),
                        reads=[buf(("x2c", xs)), buf(("r3", ch)), buf(("n3", ch))],
                        writes=[buf("SCR")])
                    B.group("tensor", [
                        (lambda tt, ic=ic: tt.transpose(
                            psT[:, ic * 128:(ic + 1) * 128],
                            SCR[:, ic * 128:(ic + 1) * 128], eye_f32[:]))
                        for ic in range(8)],
                        reads=[buf("SCR"), buf("eyef")], writes=[buf("psT")])
                    B.op("vector", lambda v: v.tensor_scalar(
                        Q1[:], psT[:], shx_bc[:], 224.0, OP.mult, OP.min),
                        reads=[buf("psT"), buf("shx")], writes=[buf("Q1")])
                    B.op("vector", lambda v, qs=qs: v.tensor_scalar(
                        XQT2[:, qs * D:(qs + 1) * D], Q1[:], -224.0, None, OP.max),
                        reads=[buf("Q1")], writes=[buf(("xqt", qs))])
                elif k == 1:
                    for oc2 in range(2):
                        B.group("tensor", [
                            (lambda tt, ic=ic, oc2=oc2, qs=qs: tt.matmul(
                                psF[2 * qs + oc2][:],
                                XQT2[:, qs * D + ic * 128: qs * D + ic * 128 + 128],
                                WFQ[:, ic * D + oc2 * 512: ic * D + oc2 * 512 + 512],
                                start=(ic == 0), stop=(ic == 7)))
                            for ic in range(8)],
                            reads=[buf(("xqt", qs))] + [buf(("WFQ", i)) for i in range(8)],
                            writes=[buf(("psF", 2 * qs + oc2))])
                elif k == 2:
                    for oc2 in range(2):
                        B.op("vector", lambda v, oc2=oc2, qs=qs, o2=o2:
                             v.scalar_tensor_tensor(
                                o2[:, oc2 * 512:(oc2 + 1) * 512],
                                psF[2 * qs + oc2][:], d_bc[:],
                                bbc[:, oc2 * 512:(oc2 + 1) * 512], OP.mult, OP.add),
                            reads=[buf(("psF", 2 * qs + oc2)), buf("dbc"), buf("bbc")],
                            writes=[buf(("o2r", qs))])
                    B.op("vector", lambda v, ch=ch, o2=o2: v.tensor_reduce(
                        stats["s1"][:, ch:ch + 1], o2, AX.X, OP.add),
                        reads=[buf(("o2r", qs))], writes=[buf(("s1", ch))])
                    B.op("vector", lambda v, ch=ch, o2=o2: v.scalar_tensor_tensor(
                        DMP[:], o2, 1.0, o2, OP.mult, OP.mult,
                        accum_out=stats["s2"][:, ch:ch + 1]),
                        reads=[buf(("o2r", qs))],
                        writes=[buf("DMP"), buf(("s2", ch))])
                    chain_dve(ch, "r4", "n4")
                    chain_act(ch, "r4")
                else:
                    chain_post(ch, "r4", "n4")
                    B.op("vector", lambda v, ch=ch, xa=xa: v.tensor_scalar(
                        xa, xa, stats["n4"][:, ch:ch + 1], None, OP.add),
                        reads=[buf(("x2c", xs)), buf(("n4", ch))],
                        writes=[buf(("x2c", xs))])
                    B.op("vector", lambda v, ch=ch, xa=xa, o2=o2: v.scalar_tensor_tensor(
                        xa, o2, stats["r4"][:, ch:ch + 1], xa, OP.mult, OP.add),
                        reads=[buf(("o2r", qs)), buf(("r4", ch)), buf(("x2c", xs))],
                        writes=[buf(("x2c", xs))])
                    dma("sync", yr[ch], xa, ("y", ch), "yw",
                        extra_reads=[("x2c", xs)])

            for step in range(NCH + 4):
                for k in (3, 2, 1, 0):
                    ch = step - k
                    if 0 <= ch < NCH:
                        stageC(ch, k)

        yq = B.dma_ctr("yw")
        x2q = B.dma_ctr("x2w")
        ytotal, x2total = yq.n, x2q.n

        def fn(sync):
            sync.wait_ge(yq.sem, ytotal)
            sync.wait_ge(x2q.sem, x2total)
        B.raw("sync", fn)

        block = st.enter_context(nc.Block())
        for ename, attr in (("sync", block.sync), ("vector", block.vector),
                            ("scalar", block.scalar), ("tensor", block.tensor),
                            ("gpsimd", block.gpsimd)):
            items = [fn for e, fn in B.script if e == ename]

            def run(eng, items=items):
                for fn in items:
                    fn(eng)
            attr(run)
    return nc


_NC_CACHE = {}


def _get_nc():
    if "nc" not in _NC_CACHE:
        nc = build_nc()
        lower_extended_insts(nc)
        _NC_CACHE["nc"] = nc
    return _NC_CACHE["nc"]


def kernel(L0_attn_query, L0_attn_mask, L0_attn_q_w, L0_attn_q_b, L0_attn_k_w,
           L0_attn_k_b, L0_attn_v_w, L0_attn_v_b, L0_attn_out_w, L0_attn_out_b,
           L0_ffn_weight, L0_ffn_bias):
    import ml_dtypes
    x = np.ascontiguousarray(np.asarray(L0_attn_query, dtype=np.float32))
    com = {
        "wqT": np.ascontiguousarray(np.asarray(L0_attn_q_w, np.float32).T),
        "wkT": np.ascontiguousarray(np.asarray(L0_attn_k_w, np.float32).T),
        "wvT": np.ascontiguousarray(np.asarray(L0_attn_v_w, np.float32).T),
        "woT": np.ascontiguousarray(np.asarray(L0_attn_out_w, np.float32).T),
        "wfT": np.ascontiguousarray(np.asarray(L0_ffn_weight, np.float32).T),
        "qb8": np.ascontiguousarray(np.asarray(L0_attn_q_b, np.float32).reshape(8, 128).T),
        "kb8": np.ascontiguousarray(np.asarray(L0_attn_k_b, np.float32).reshape(8, 128).T),
        "vbrow": np.asarray(L0_attn_v_b, np.float32).reshape(1, D),
        "obrow": np.asarray(L0_attn_out_b, np.float32).reshape(1, D),
        "fbrow": np.asarray(L0_ffn_bias, np.float32).reshape(1, D),
        "eye_bf": np.eye(128, dtype=np.float32).astype(ml_dtypes.bfloat16),
        "eye_f32": np.eye(128, dtype=np.float32),
        "ones32": np.ones((128, 64), dtype=np.float32).astype(ml_dtypes.bfloat16),
    }
    nc = _get_nc()
    in_maps = [dict(com, x=np.ascontiguousarray(x[b])) for b in range(8)]
    res = run_bass_kernel_spmd(nc, in_maps, core_ids=list(range(8)))
    out = np.stack([np.asarray(res.results[b]["y"]) for b in range(8)], axis=0)
    return out
